# version 20
# baseline (speedup 1.0000x reference)
import numpy as np
import concourse.bass as bass
import concourse.mybir as mybir
from concourse.bass_utils import run_bass_kernel_spmd

F32 = mybir.dt.float32
BF16 = mybir.dt.bfloat16
AF = mybir.ActivationFunctionType
ALU = mybir.AluOpType
AX = mybir.AxisListType

S = 2048
E = 1024
F = 8
H, HD = 16, 64
NL = 4
NE = 16
EPS = 1e-5
CA = 256
CB = 512
NEG = -1.0e30

DEBUG_TAPS = []
SIM_STRICT = False


def _build_constants():
    import ml_dtypes
    ks = np.fft.fftfreq(S)
    kh = np.fft.fftfreq(H)
    kd = np.fft.fftfreq(HD)
    km = np.sqrt(ks[:, None, None] ** 2 + kh[None, :, None] ** 2
                 + kd[None, None, :] ** 2 + 1e-10)
    c = np.cos(np.arctan(np.log(km + 1e-10)))
    hh = np.arange(H)[:, None]
    dd = np.arange(HD)[None, :]
    cols, eigs, seen = [], [], set()
    for fh in range(H):
        for fd in range(HD):
            if (fh, fd) in seen:
                continue
            ph, pd = (-fh) % H, (-fd) % HD
            seen.add((fh, fd)); seen.add((ph, pd))
            phase = 2 * np.pi * (fh * hh / H + fd * dd / HD)
            if (ph, pd) == (fh, fd):
                cols.append((np.cos(phase) / np.sqrt(E)).reshape(-1))
                eigs.append(c[:, fh, fd])
            else:
                cols.append((np.cos(phase) * np.sqrt(2.0 / E)).reshape(-1))
                eigs.append(c[:, fh, fd])
                cols.append((np.sin(phase) * np.sqrt(2.0 / E)).reshape(-1))
                eigs.append(c[:, fh, fd])
    Q = np.stack(cols, axis=1)
    ctil = np.stack(eigs, axis=0)
    lam = np.arange(S) / S
    t = np.arange(S) * 0.1
    mod = 1.0 + 0.1 * (np.sin(t + lam) * np.cos(t - 2.0 * lam + 0.5 * lam * lam))
    return {
        "cQ": np.ascontiguousarray(Q.astype(np.float32).astype(ml_dtypes.bfloat16)),
        "cCT": np.ascontiguousarray(ctil.astype(np.float32)),
        "cMOD": np.ascontiguousarray(np.broadcast_to(
            mod.astype(np.float32)[None, :], (128, S))),
        "cID": np.eye(128, dtype=np.float32),
        "cONEB": np.ones((128, 1), dtype=ml_dtypes.bfloat16),
        "cONER": np.ones((1, 128), dtype=np.float32),
    }


def _perm_qkv():
    idx = np.arange(3 * E).reshape(H, 3, HD)
    return np.concatenate([idx[:, 0, :].reshape(-1),
                           idx[:, 1, :].reshape(-1),
                           idx[:, 2, :].reshape(-1)])


def _cols(v, ntiles):
    return np.ascontiguousarray(np.asarray(v, np.float32).reshape(ntiles, 128).T)


class Ctr:
    def __init__(self, nc, name, step):
        self.h = nc.alloc_semaphore(name)
        self.n = 0
        self.step = step
        self.last = None

    def inc(self, instr):
        if instr is self.last:
            return self.n
        instr.then_inc(self.h, self.step)
        self.n += self.step
        self.last = instr
        return self.n


_AUTO_OPS = {
    "reciprocal",
    "tensor_tensor", "tensor_scalar", "tensor_single_scalar", "tensor_copy",
    "tensor_add", "tensor_sub", "tensor_mul", "tensor_max", "memset", "memzero",
    "reciprocal", "reduce_max", "reduce_sum", "tensor_reduce",
    "scalar_tensor_tensor", "select", "activation", "copy", "mul", "add",
    "square", "sqrt", "sign",
}


class MarkedEng:
    def __init__(self, eng, ctr):
        self._e = eng
        self._c = ctr

    def __getattr__(self, name):
        f = getattr(self._e, name)
        if name in _AUTO_OPS:
            c = self._c
            eng = self._e
            always = name == "reciprocal"

            def g(*a, **k):
                i = f(*a, **k)
                c.inc(i)
                small = SIM_STRICT
                if not small:
                    try:
                        small = a[0].free_size() < 256 or a[0].partition_size() == 1
                    except Exception:
                        small = True
                if always or small:
                    eng.wait_ge(c.h, c.n)
                return i
            return g
        return f


def build_program():
    nc = bass.Bass("TRN2", target_bir_lowering=False)
    sync, ten, vec, act, pool = nc.sync, nc.tensor, nc.vector, nc.scalar, nc.gpsimd

    def din(name, shape, dtype=F32):
        return nc.dram_tensor(name, list(shape), dtype, kind="ExternalInput").ap()

    x_h = din("x", (S, E))
    aw_h = [din(f"aw{l}", (E, 3 * E)) for l in range(NL)]
    ab_h = [din(f"ab{l}", (128, 24)) for l in range(NL)]
    pw_h = [din(f"pw{l}", (E, E)) for l in range(NL)]
    pb_h = [din(f"pb{l}", (128, 8)) for l in range(NL)]
    fw_h = [din(f"fw{l}", (E, 4 * E)) for l in range(NL)]
    fb_h = [din(f"fb{l}", (128, 32)) for l in range(NL)]
    mw_h = [din(f"mw{l}", (4 * E, E)) for l in range(NL)]
    mb_h = [din(f"mb{l}", (128, 8)) for l in range(NL)]
    l1w_h = [din(f"l1w{l}", (128, 8)) for l in range(NL)]
    l1b_h = [din(f"l1b{l}", (128, 8)) for l in range(NL)]
    l2w_h = [din(f"l2w{l}", (128, 8)) for l in range(NL)]
    l2b_h = [din(f"l2b{l}", (128, 8)) for l in range(NL)]
    gk_h = din("gk", (E, NE))
    gb_h = din("gb", (NE, 1))
    wg_h = din("wg", (NE, E, 512))
    wu_h = din("wu", (NE, E, 512))
    wd_h = din("wd", (NE, 512, E))
    swg_h = din("swg", (E, E))
    swu_h = din("swu", (E, E))
    swd_h = din("swd", (E, E))
    cQ_h = din("cQ", (E, E), BF16)
    cCT_h = din("cCT", (E, S))
    cMOD_h = din("cMOD", (128, S))
    cID_h = din("cID", (128, 128))
    cONEB_h = din("cONEB", (128, 1), BF16)
    cONER_h = din("cONER", (1, 128))

    out_h = nc.dram_tensor("out", [S, E], F32, kind="ExternalOutput").ap()

    def scr(name, shape, dtype=F32):
        return nc.dram_tensor(name, list(shape), dtype, kind="Internal").ap()

    xT_s = scr("xT_s", (F, 128, S))
    xa_s = scr("xa_s", (F, 128, S))
    xnb_s = scr("xnb_s", (F, 128, S), BF16)
    p0_s = scr("p0_s", (F, 128, S), BF16)
    comb_s = scr("comb_s", (NE, S))
    ln1_s = scr("ln1_s", (F, 128, S), BF16) if "ln1_0" in DEBUG_TAPS else None
    qkv_s = scr("qkv_s", (24, 128, S)) if "qkv_0" in DEBUG_TAPS else None

    taps = {}
    _tapshape = {"comb": [NE, S], "qkv_0": [24, 128, S], "sfcT": [128, 256],
                 "ms": [128, 256], "rcm": [128, 256]}
    for t in DEBUG_TAPS:
        shp = _tapshape.get(t, [F, 128, S])
        taps[t] = nc.dram_tensor("tap_" + t, shp, F32, kind="ExternalOutput").ap()

    LD = Ctr(nc, "cLD", 16)
    WL = Ctr(nc, "cWL", 16)
    TEN = Ctr(nc, "cTEN", 1)
    VEC = Ctr(nc, "cVEC", 1)
    ACTC = Ctr(nc, "cACT", 1)
    POOLC = Ctr(nc, "cPOOL", 1)
    vec = MarkedEng(vec, VEC)
    act = MarkedEng(act, ACTC)
    plc = MarkedEng(pool, POOLC)

    def ld(dst, src):
        if SIM_STRICT:
            sync.wait_ge(LD.h, LD.n)
        sync.dma_start(dst, src).then_inc(LD.h, 16)
        LD.n += 16

    def wld(dst, src):
        if SIM_STRICT:
            pool.wait_ge(WL.h, WL.n)
        pool.dma_start(dst, src).then_inc(WL.h, 16)
        WL.n += 16

    def bar():
        nc.all_engine_barrier()

    import contextlib as _ctx

    @_ctx.contextmanager
    def scope(nm):
        with nc.named_scope(nm, notify=True):
            yield

    def stage_start():
        for e in (sync, pool, ten, nc.vector, nc.scalar):
            e.wait_ge(LD.h, LD.n)
            e.wait_ge(WL.h, WL.n)

    class Bank:
        __slots__ = ("ps", "a", "v")
        def __init__(self, ps):
            self.ps = ps
            self.a = 0
            self.v = 0

        def wait(self):
            if self.a:
                ten.wait_ge(ACTC.h, self.a)
            if self.v:
                ten.wait_ge(VEC.h, self.v)

    def tap3(name, src_scr):
        if name in taps:
            bar()
            pool.wait_ge(LD.h, LD.n)
            wld(taps[name][:], src_scr[:])
            pool.wait_ge(WL.h, WL.n)
            bar()

    import contextlib
    stack = contextlib.ExitStack()
    _names = {}

    def uname(n):
        _names[n] = _names.get(n, -1) + 1
        return f"{n}_{_names[n]}" if _names[n] else n

    def sb(name, shape, dtype=F32):
        return stack.enter_context(nc.sbuf_tensor(name, list(shape), dtype))

    ident = sb("ident", (128, 128))
    oneb = sb("oneb", (128, 1), BF16)
    oner = sb("oner", (1, 128))
    logits = sb("logits", (NE, S))
    rows_mu = sb("rows_mu", (1, 520))
    rows_rs = sb("rows_rs", (1, 520))
    rows_tp = sb("rows_tp", (1, 520))

    epsv = sb("epsv", (128, 1))
    vec.memset(epsv[:], float(EPS))
    VEC.inc(vec.engine_nop()) if False else None
    ld(ident[:], cID_h[:])
    ld(oneb[:], cONEB_h[:])
    ld(oner[:], cONER_h[:])

    with contextlib.ExitStack() as stg, scope("IN"):
        in_rows = stg.enter_context(nc.sbuf_tensor(uname("in_rows"), [128, E], F32))
        in_cols = stg.enter_context(nc.sbuf_tensor(uname("in_cols"), [128, F, 128], F32))
        in_ps = stg.enter_context(nc.psum_tensor(uname("in_ps"), [128, 128], F32))
        for t in range(S // 128):
            sync.wait_ge(VEC.h, VEC.n)
            ld(in_rows[:], x_h[t * 128:(t + 1) * 128, :])
            ten.wait_ge(LD.h, LD.n)
            for f in range(F):
                if f > 0:
                    ten.wait_ge(VEC.h, VEC.n)
                TEN.inc(ten.transpose(in_ps[:], in_rows[:, f * 128:(f + 1) * 128],
                                      ident[:]))
                vec.wait_ge(TEN.h, TEN.n)
                VEC.inc(vec.tensor_copy(in_cols[:, f, :], in_ps[:]))
            sync.wait_ge(VEC.h, VEC.n)
            ld(xT_s[:, :, t * 128:(t + 1) * 128].transpose([1, 0, 2]), in_cols[:])
    bar()

    def ln_stats_top(src3, C, xb, xq, psA, psB):
        act.wait_ge(VEC.h, VEC.n)
        act.activation(xb, src3, AF.Copy)
        act.activation(xq, src3, AF.Square)
        ten.wait_ge(ACTC.h, ACTC.n)
        ten.wait_ge(VEC.h, VEC.n)
        for f in range(F):
            ten.matmul(psA[:, :C], oneb[:], xb[:, f, :], start=(f == 0), stop=(f == F - 1))
        i = None
        for f in range(F):
            i = ten.matmul(psB[:, :C], oneb[:], xq[:, f, :], start=(f == 0), stop=(f == F - 1))
        TEN.inc(i)

    def ln_stats_fin(C, psA, psB, psR, psM):
        mu = rows_mu[:, 0:C]
        rs = rows_rs[:, 0:C]
        tp = rows_tp[:, 0:C]
        vec.wait_ge(TEN.h, TEN.n)
        vec.tensor_scalar(mu, psA[:, :C], 1.0 / E, None, ALU.mult)
        vec.tensor_scalar(tp, psB[:, :C], 1.0 / E, None, ALU.mult)
        vec.tensor_tensor(rs, mu, mu, ALU.mult)
        vec.tensor_tensor(tp, tp, rs, ALU.subtract)
        act.wait_ge(VEC.h, VEC.n)
        act.activation(tp, tp, AF.Ln, bias=epsv[0:1, 0:1])
        act.activation(rs, tp, AF.Exp, scale=-0.5)
        vec.wait_ge(ACTC.h, ACTC.n)
        vec.tensor_tensor(mu, mu, rs, ALU.mult)
        ten.wait_ge(VEC.h, VEC.n)
        ten.matmul(psR[:, :C], oner[:], rs, start=True, stop=True)
        TEN.inc(ten.matmul(psM[:, :C], oner[:], mu, start=True, stop=True))
        vec.wait_ge(TEN.h, TEN.n)
        psRb = psR[:, :C].unsqueeze(1).broadcast_to([128, F, C])
        psMb = psM[:, :C].unsqueeze(1).broadcast_to([128, F, C])
        return psRb, psMb

    def ln_stats(src3, C, xb, xq, psA, psB, psR, psM):
        ln_stats_top(src3, C, xb, xq, psA, psB)
        return ln_stats_fin(C, psA, psB, psR, psM)

    with contextlib.ExitStack() as stg, scope("OLN"):
        olx = stg.enter_context(nc.sbuf_tensor(uname("olx"), [128, F, CB], F32))
        olo = stg.enter_context(nc.sbuf_tensor(uname("olo"), [128, F, CB], F32))
        olob = stg.enter_context(nc.sbuf_tensor(uname("olob"), [128, F, CB], BF16))
        olxb = stg.enter_context(nc.sbuf_tensor(uname("olxb"), [128, F, CB], BF16))
        olxq = stg.enter_context(nc.sbuf_tensor(uname("olxq"), [128, F, CB], BF16))
        gksb = stg.enter_context(nc.sbuf_tensor(uname("gksb"), [128, F, NE], F32))
        ol_psA = stg.enter_context(nc.psum_tensor(uname("ol_psA"), [1, CB], F32))
        ol_psB = stg.enter_context(nc.psum_tensor(uname("ol_psB"), [1, CB], F32))
        ol_psR = stg.enter_context(nc.psum_tensor(uname("ol_psR"), [128, CB], F32))
        ol_psM = stg.enter_context(nc.psum_tensor(uname("ol_psM"), [128, CB], F32))
        ol_psG = stg.enter_context(nc.psum_tensor(uname("ol_psG"), [NE, CB], F32))
        stage_start()
        ld(gksb[:], gk_h.rearrange("(k p) m -> p k m", p=128))
        for ci in range(S // CB):
            c0 = ci * CB
            sync.wait_ge(VEC.h, VEC.n)
            sync.wait_ge(ACTC.h, ACTC.n)
            ld(olx[:], xT_s[:, :, c0:c0 + CB].transpose([1, 0, 2]))
            for e in (act, ten, vec):
                e.wait_ge(LD.h, LD.n)
            psRb, psMb = ln_stats(olx[:], CB, olxb[:], olxq[:],
                                  ol_psA, ol_psB, ol_psR, ol_psM)
            vec.tensor_tensor(olo[:], olx[:], psRb, ALU.mult)
            VEC.inc(vec.tensor_tensor(olo[:], olo[:], psMb, ALU.subtract))
            act.wait_ge(VEC.h, VEC.n)
            ACTC.inc(act.activation(olob[:], olo[:], AF.Copy))
            ten.wait_ge(VEC.h, VEC.n)
            i = None
            for f in range(F):
                i = ten.matmul(ol_psG[:, :CB], gksb[:, f, :], olo[:, f, :],
                               start=(f == 0), stop=(f == F - 1))
            TEN.inc(i)
            vec.wait_ge(TEN.h, TEN.n)
            VEC.inc(vec.tensor_copy(logits[:, c0:c0 + CB], ol_psG[:, :CB]))
            sync.wait_ge(VEC.h, VEC.n)
            sync.wait_ge(ACTC.h, ACTC.n)
            ld(xa_s[:, :, c0:c0 + CB].transpose([1, 0, 2]), olo[:])
            ld(xnb_s[:, :, c0:c0 + CB].transpose([1, 0, 2]), olob[:])
    bar()
    tap3("xn", xa_s)

    for l in range(NL):
        CAs = [128] * 16
        C0s = [sum(CAs[:j]) for j in range(len(CAs))]
        NCH = len(CAs)
        WMAX = 129
        with contextlib.ExitStack() as stg, scope(f"A{l}"):
            aw_sb = stg.enter_context(nc.sbuf_tensor(uname("aw_sb"), [128, F, 3 * E], BF16))
            abv = stg.enter_context(nc.sbuf_tensor(uname("abv"), [128, 24], F32))
            l1wv = stg.enter_context(nc.sbuf_tensor(uname("l1wv"), [128, 8], F32))
            l1bv = stg.enter_context(nc.sbuf_tensor(uname("l1bv"), [128, 8], F32))
            ax = stg.enter_context(nc.sbuf_tensor(uname("ax"), [128, F, WMAX], F32))
            axb = stg.enter_context(nc.sbuf_tensor(uname("axb"), [128, F, WMAX], BF16))
            axq = stg.enter_context(nc.sbuf_tensor(uname("axq"), [128, F, WMAX], BF16))
            lnb = stg.enter_context(nc.sbuf_tensor(uname("lnb"), [128, F, WMAX], BF16))
            qkv2 = [stg.enter_context(nc.sbuf_tensor(uname("qkvA"), [128, 24, WMAX], F32)),
                    stg.enter_context(nc.sbuf_tensor(uname("qkvB"), [128, 24, WMAX], F32))]
            p0b2 = [stg.enter_context(nc.sbuf_tensor(uname("p0bA"), [128, F, 128], BF16)),
                    stg.enter_context(nc.sbuf_tensor(uname("p0bB"), [128, F, 128], BF16))]
            def pair(nm):
                return [stg.enter_context(nc.sbuf_tensor(uname(nm), [128, F, 128], F32))
                        for _ in range(2)]
            tq3_2 = pair("tq3")
            tk3_2 = pair("tk3")
            tv3_2 = pair("tv3")
            trlq_2 = pair("trlq")
            trlk_2 = pair("trlk")
            trlv_2 = pair("trlv")
            tq1 = stg.enter_context(nc.sbuf_tensor(uname("tq1"), [128, F, 128], F32))
            tk1 = stg.enter_context(nc.sbuf_tensor(uname("tk1"), [128, F, 128], F32))
            tm = stg.enter_context(nc.sbuf_tensor(uname("tm"), [128, F, 128], F32))
            tt = stg.enter_context(nc.sbuf_tensor(uname("tt"), [128, F, 128], F32))
            att = stg.enter_context(nc.sbuf_tensor(uname("att"), [128, F, 128], F32))
            qk0 = stg.enter_context(nc.sbuf_tensor(uname("qk0"), [128, F, 128], F32))
            qk1 = stg.enter_context(nc.sbuf_tensor(uname("qk1"), [128, F, 128], F32))
            qk2 = stg.enter_context(nc.sbuf_tensor(uname("qk2"), [128, F, 128], F32))
            qk3 = stg.enter_context(nc.sbuf_tensor(uname("qk3"), [128, F, 128], F32))
            a_psA = stg.enter_context(nc.psum_tensor(uname("a_psA"), [1, WMAX], F32))
            a_psB = stg.enter_context(nc.psum_tensor(uname("a_psB"), [1, WMAX], F32))
            a_psR = stg.enter_context(nc.psum_tensor(uname("a_psR"), [128, WMAX], F32))
            a_psM = stg.enter_context(nc.psum_tensor(uname("a_psM"), [128, WMAX], F32))
            a_psQ = [stg.enter_context(nc.psum_tensor(uname("a_psQ"), [128, WMAX], F32))
                     for _ in range(4)]
            stage_start()
            wld(aw_sb[:], aw_h[l].rearrange("(k p) m -> p k m", p=128))
            ld(abv[:], ab_h[l][:])
            ld(l1wv[:], l1w_h[l][:])
            ld(l1bv[:], l1b_h[l][:])
            psQ = [Bank(p) for p in a_psQ]
            evac_marks = [0, 0]
            prep_marks = [0, 0]
            cos_marks = [0, 0]
            p0_mark = [0, 0]
            lnb_mark = [0]

            def front_top(j):
                Cj = CAs[j]
                c0 = C0s[j]
                hl = 1 if j > 0 else 0
                W = Cj + hl
                sync.wait_ge(VEC.h, VEC.n)
                sync.wait_ge(ACTC.h, ACTC.n)
                ld(ax[:, :, :W], xa_s[:, :, c0 - hl:c0 + Cj].transpose([1, 0, 2]))
                for e in (act, ten):
                    e.wait_ge(LD.h, LD.n)
                    e.wait_ge(WL.h, WL.n)
                ln_stats_top(ax[:, :, :W], W, axb[:, :, :W], axq[:, :, :W],
                             a_psA, a_psB)

            def front_dve(j):
                Cj = CAs[j]
                hl = 1 if j > 0 else 0
                W = Cj + hl
                psRb, psMb = ln_stats_fin(W, a_psA, a_psB, a_psR, a_psM)
                gam = l1wv[:].unsqueeze(2).broadcast_to([128, F, W])
                bet = l1bv[:].unsqueeze(2).broadcast_to([128, F, W])
                vec.tensor_tensor(ax[:, :, :W], ax[:, :, :W], psRb, ALU.mult)
                vec.tensor_tensor(ax[:, :, :W], ax[:, :, :W], psMb, ALU.subtract)
                vec.tensor_tensor(ax[:, :, :W], ax[:, :, :W], gam, ALU.mult)
                vec.tensor_tensor(lnb[:, :, :W], ax[:, :, :W], bet, ALU.add)
                lnb_mark[0] = VEC.n

            def front_mm(j):
                Cj = CAs[j]
                c0 = C0s[j]
                hl = 1 if j > 0 else 0
                W = Cj + hl
                par = j % 2
                qkv = qkv2[par]
                ten.wait_ge(VEC.h, lnb_mark[0])
                for m in range(24):
                    bk = psQ[m % 4]
                    bk.wait()
                    i = None
                    for k in range(F):
                        i = ten.matmul(bk.ps[:, :W], aw_sb[:, k, m * 128:(m + 1) * 128],
                                       lnb[:, k, :W], start=(k == 0), stop=(k == F - 1))
                    TEN.inc(i)
                    act.wait_ge(TEN.h, TEN.n)
                    if m == 0 and p0_mark[par]:
                        act.wait_ge(VEC.h, p0_mark[par])
                    act.activation(qkv[:, m, :W], bk.ps[:, :W], AF.Identity,
                                   bias=abv[:, m:m + 1])
                    bk.a = ACTC.n
                evac_marks[par] = ACTC.n
                q0 = qkv[:, 0:8, hl:W]
                k0 = qkv[:, 8:16, hl:W]
                v0 = qkv[:, 16:24, hl:W]

                def act_cos(dst, src_):
                    act.activation(att[:, :, :Cj], src_, AF.Sin, scale=0.5)
                    act.activation(att[:, :, :Cj], att[:, :, :Cj], AF.Square)
                    act.activation(dst[:, :, :Cj], att[:, :, :Cj], AF.Identity,
                                   scale=-2.0, bias=1.0)

                def act_roll(dst, zb):
                    act.copy(dst[64:128, :, :Cj], zb[0:64, :, :])
                    act.copy(dst[0:64, 1:8, :Cj], zb[64:128, 0:7, :])
                    act.copy(dst[0:64, 0:1, :Cj], zb[64:128, 7:8, :])

                if p0_mark[par]:
                    act.wait_ge(VEC.h, p0_mark[par])
                act_cos(tq3_2[par], q0)
                act_cos(tk3_2[par], k0)
                cos_marks[par] = ACTC.n
                act_roll(trlq_2[par], q0)
                act_roll(trlk_2[par], k0)
                act_cos(tv3_2[par], v0)
                act_roll(trlv_2[par], v0)
                prep_marks[par] = ACTC.n

            def quat_dve(i, half):
                Ci = CAs[i]
                c0 = C0s[i]
                hl = 1 if i > 0 else 0
                W = Ci + hl
                par = i % 2
                qkv = qkv2[par]
                p0b = p0b2[par]
                q0 = qkv[:, 0:8, hl:W]
                k0 = qkv[:, 8:16, hl:W]
                v0 = qkv[:, 16:24, hl:W]
                tq3, tk3, tv3 = tq3_2[par], tk3_2[par], tv3_2[par]
                trlq, trlk, trlv = trlq_2[par], trlk_2[par], trlv_2[par]
                T = lambda t: t[:, :, :Ci]

                def diff(dst, base):
                    if hl:
                        vec.tensor_tensor(T(dst), qkv[:, base:base + 8, 1:W],
                                          qkv[:, base:base + 8, 0:W - 1], ALU.subtract)
                    else:
                        vec.tensor_tensor(dst[:, :, 1:Ci], qkv[:, base:base + 8, 1:Ci],
                                          qkv[:, base:base + 8, 0:Ci - 1], ALU.subtract)
                        vec.memset(dst[:, :, 0:1], 0.0)

                def roll_into(dst, zb):
                    vec.tensor_copy(dst[64:128, :, :Ci], zb[0:64, :, :])
                    vec.tensor_copy(dst[0:64, 1:8, :Ci], zb[64:128, 0:7, :])
                    vec.tensor_copy(dst[0:64, 0:1, :Ci], zb[64:128, 7:8, :])

                if half == 0:
                    vec.wait_ge(ACTC.h, evac_marks[par])
                    diff(tq1, 0)
                    diff(tk1, 8)
                    vec.tensor_tensor(T(tm), q0, k0, ALU.mult)
                    vec.tensor_tensor(T(tt), T(tq1), T(tk1), ALU.mult)
                    vec.tensor_tensor(T(qk0), T(tm), T(tt), ALU.subtract)
                    vec.wait_ge(ACTC.h, prep_marks[par])
                    vec.tensor_tensor(T(qk1), T(trlq), T(trlk), ALU.mult)
                    vec.tensor_tensor(T(qk0), T(qk0), T(qk1), ALU.subtract)
                    vec.tensor_tensor(T(tt), T(tq3), T(tk3), ALU.mult)
                    vec.tensor_tensor(T(qk0), T(qk0), T(tt), ALU.subtract)
                    vec.tensor_tensor(T(qk1), q0, T(tk1), ALU.mult)
                    vec.tensor_tensor(T(tt), T(tq1), k0, ALU.mult)
                    vec.tensor_tensor(T(qk1), T(qk1), T(tt), ALU.add)
                    vec.tensor_tensor(T(tt), T(trlq), T(tk3), ALU.mult)
                    vec.tensor_tensor(T(qk1), T(qk1), T(tt), ALU.add)
                    vec.tensor_tensor(T(tt), T(tq3), T(trlk), ALU.mult)
                    vec.tensor_tensor(T(qk1), T(qk1), T(tt), ALU.subtract)
                else:
                    vec.tensor_tensor(T(qk2), q0, T(trlk), ALU.mult)
                    vec.tensor_tensor(T(tt), T(tq1), T(tk3), ALU.mult)
                    vec.tensor_tensor(T(qk2), T(qk2), T(tt), ALU.subtract)
                    vec.tensor_tensor(T(tt), T(trlq), k0, ALU.mult)
                    vec.tensor_tensor(T(qk2), T(qk2), T(tt), ALU.add)
                    vec.tensor_tensor(T(tt), T(tq3), T(tk1), ALU.mult)
                    vec.tensor_tensor(T(qk2), T(qk2), T(tt), ALU.add)
                    vec.tensor_tensor(T(qk3), q0, T(tk3), ALU.mult)
                    vec.tensor_tensor(T(tt), T(tq1), T(trlk), ALU.mult)
                    vec.tensor_tensor(T(qk3), T(qk3), T(tt), ALU.add)
                    vec.tensor_tensor(T(tt), T(trlq), T(tk1), ALU.mult)
                    vec.tensor_tensor(T(qk3), T(qk3), T(tt), ALU.subtract)
                    vec.tensor_tensor(T(tt), T(tq3), k0, ALU.mult)
                    vec.tensor_tensor(T(qk3), T(qk3), T(tt), ALU.add)
                    diff(tq1, 16)
                    vec.tensor_tensor(T(tm), T(qk0), v0, ALU.mult)
                    vec.tensor_tensor(T(tt), T(qk1), T(tq1), ALU.mult)
                    vec.tensor_tensor(T(tm), T(tm), T(tt), ALU.subtract)
                    vec.tensor_tensor(T(tt), T(qk2), T(trlv), ALU.mult)
                    vec.tensor_tensor(T(tm), T(tm), T(tt), ALU.subtract)
                    vec.tensor_tensor(T(tt), T(qk3), T(tv3), ALU.mult)
                    vec.wait_ge(LD.h, LD.n)
                    vec.tensor_tensor(p0b[:, :, :Ci], T(tm), T(tt), ALU.subtract)
                    p0_mark[par] = VEC.n
                    sync.wait_ge(VEC.h, VEC.n)
                    ld(p0_s[:, :, c0:c0 + Ci].transpose([1, 0, 2]), p0b[:, :, :Ci])

            front_top(0)
            front_dve(0)
            front_mm(0)
            for i in range(NCH):
                if i + 1 < NCH:
                    front_top(i + 1)
                quat_dve(i, 0)
                if i + 1 < NCH:
                    front_dve(i + 1)
                quat_dve(i, 1)
                if i + 1 < NCH:
                    front_mm(i + 1)
        bar()
        if l == 0:
            tap3("p0_0", p0_s)
            if ln1_s is not None:
                tap3("ln1_0", ln1_s)
            if qkv_s is not None:
                tap3("qkv_0", qkv_s)

        with contextlib.ExitStack() as stg, scope(f"B{l}"):
            Qsb = stg.enter_context(nc.sbuf_tensor(uname("Qsb"), [128, F, E], BF16))
            pw_sb = stg.enter_context(nc.sbuf_tensor(uname("pw_sb"), [128, F, E], BF16))
            wp_sb = stg.enter_context(nc.sbuf_tensor(uname("wp_sb"), [128, F, E], BF16))
            pbv = stg.enter_context(nc.sbuf_tensor(uname("pbv"), [128, 8], F32))
            modb = stg.enter_context(nc.sbuf_tensor(uname("modb"), [128, S], F32))
            bp0 = stg.enter_context(nc.sbuf_tensor(uname("bp0"), [128, F, CB], BF16))
            bct = stg.enter_context(nc.sbuf_tensor(uname("bct"), [128, F, CB], F32))
            bxa = stg.enter_context(nc.sbuf_tensor(uname("bxa"), [128, F, CB], F32))
            btp = stg.enter_context(nc.sbuf_tensor(uname("btp"), [128, F, CB], BF16))
            b_ps0 = stg.enter_context(nc.psum_tensor(uname("b_ps0"), [128, CB], F32))
            b_ps1 = stg.enter_context(nc.psum_tensor(uname("b_ps1"), [128, CB], F32))
            stage_start()
            ld(Qsb[:], cQ_h.rearrange("(k p) m -> p k m", p=128))
            ld(modb[:], cMOD_h[:])
            wld(pw_sb[:], pw_h[l].rearrange("(k p) m -> p k m", p=128))
            ld(pbv[:], pb_h[l][:])
            psB2 = [Bank(b_ps0), Bank(b_ps1)]
            ten.wait_ge(WL.h, WL.n)
            ten.wait_ge(LD.h, LD.n)
            for m in range(F):
                for half in range(2):
                    bk = psB2[half]
                    bk.wait()
                    n0 = half * 512
                    i = None
                    for k in range(F):
                        i = ten.matmul(bk.ps[:, :512], Qsb[:, k, m * 128:(m + 1) * 128],
                                       pw_sb[:, k, n0:n0 + 512],
                                       start=(k == 0), stop=(k == F - 1))
                    TEN.inc(i)
                    act.wait_ge(TEN.h, TEN.n)
                    bk.a = ACTC.inc(act.activation(
                        wp_sb[:, m, n0:n0 + 512], bk.ps[:, :512], AF.Copy))
            for ci in range(S // CB):
                c0 = ci * CB
                sync.wait_ge(VEC.h, VEC.n)
                sync.wait_ge(LD.h, LD.n)
                ld(bp0[:], p0_s[:, :, c0:c0 + CB].transpose([1, 0, 2]))
                ld(bct[:], cCT_h.rearrange("(f p) s -> p f s", p=128)[:, :, c0:c0 + CB])
                ld(bxa[:], xa_s[:, :, c0:c0 + CB].transpose([1, 0, 2]))
                for e in (ten, vec):
                    e.wait_ge(LD.h, LD.n)
                for m in range(F):
                    bk = psB2[m % 2]
                    bk.wait()
                    i = None
                    for k in range(F):
                        i = ten.matmul(bk.ps[:, :CB], Qsb[:, k, m * 128:(m + 1) * 128],
                                       bp0[:, k, :], start=(k == 0), stop=(k == F - 1))
                    TEN.inc(i)
                    vec.wait_ge(TEN.h, TEN.n)
                    bk.v = VEC.inc(vec.tensor_tensor(
                        btp[:, m, :], bk.ps[:, :CB], bct[:, m, :], ALU.mult))
                ten.wait_ge(VEC.h, VEC.n)
                for m in range(F):
                    bk = psB2[m % 2]
                    bk.wait()
                    i = None
                    for k in range(F):
                        i = ten.matmul(bk.ps[:, :CB], wp_sb[:, k, m * 128:(m + 1) * 128],
                                       btp[:, k, :], start=(k == 0), stop=(k == F - 1))
                    TEN.inc(i)
                    vec.wait_ge(TEN.h, TEN.n)
                    vec.tensor_tensor(bct[:, m, :], bk.ps[:, :CB],
                                      modb[:, c0:c0 + CB], ALU.mult)
                    bk.v = VEC.inc(vec.scalar_tensor_tensor(
                        bxa[:, m, :], bct[:, m, :], pbv[:, m:m + 1], bxa[:, m, :],
                        ALU.add, ALU.add))
                sync.wait_ge(VEC.h, VEC.n)
                ld(xa_s[:, :, c0:c0 + CB].transpose([1, 0, 2]), bxa[:])
        bar()
        if l == 0:
            tap3("xa_a0", xa_s)

        with contextlib.ExitStack() as stg, scope(f"D{l}"):
            fw_sb = stg.enter_context(nc.sbuf_tensor(uname("fw_sb"), [128, F, 4 * E], BF16))
            mwb = [stg.enter_context(nc.sbuf_tensor(uname("mwb"), [128, 32, 128], BF16))
                   for _ in range(3)]
            fbv = stg.enter_context(nc.sbuf_tensor(uname("fbv"), [128, 32], F32))
            mbv = stg.enter_context(nc.sbuf_tensor(uname("mbv"), [128, 8], F32))
            l2wv = stg.enter_context(nc.sbuf_tensor(uname("l2wv"), [128, 8], F32))
            l2bv = stg.enter_context(nc.sbuf_tensor(uname("l2bv"), [128, 8], F32))
            dx2 = [stg.enter_context(nc.sbuf_tensor(uname("dxA"), [128, F, CB], F32)),
                   stg.enter_context(nc.sbuf_tensor(uname("dxB"), [128, F, CB], F32))]
            dln2 = [stg.enter_context(nc.sbuf_tensor(uname("dlnA"), [128, F, CB], BF16)),
                    stg.enter_context(nc.sbuf_tensor(uname("dlnB"), [128, F, CB], BF16))]
            dxb = stg.enter_context(nc.sbuf_tensor(uname("dxb"), [128, F, CB], BF16))
            dgs = stg.enter_context(nc.sbuf_tensor(uname("dgs"), [128, F, CB], F32))
            dg = stg.enter_context(nc.sbuf_tensor(uname("dg"), [128, 32, CB], BF16))
            d_psA = stg.enter_context(nc.psum_tensor(uname("d_psA"), [1, CB], F32))
            d_psB = stg.enter_context(nc.psum_tensor(uname("d_psB"), [1, CB], F32))
            d_psR = stg.enter_context(nc.psum_tensor(uname("d_psR"), [128, CB], F32))
            d_psM = stg.enter_context(nc.psum_tensor(uname("d_psM"), [128, CB], F32))
            d_psF0 = stg.enter_context(nc.psum_tensor(uname("d_psF0"), [128, CB], F32))
            d_psF1 = stg.enter_context(nc.psum_tensor(uname("d_psF1"), [128, CB], F32))
            stage_start()
            wld(fw_sb[:], fw_h[l].rearrange("(k p) m -> p k m", p=128))
            mwT = mw_h[l].rearrange("(k p) m -> p k m", p=128)
            mw_cons = [0, 0, 0]
            mw_tok = [0, 0, 0]
            mw_next = [0]

            def mw_prefetch(upto):
                total = NCH_D * F if True else 0
                while mw_next[0] < min(upto, S // CB * F):
                    g = mw_next[0]
                    slot = g % 3
                    pool.wait_ge(TEN.h, mw_cons[slot])
                    wld(mwb[slot][:], mwT[:, :, (g % F) * 128:(g % F) * 128 + 128])
                    mw_tok[slot] = WL.n
                    mw_next[0] = g + 1
            ld(fbv[:], fb_h[l][:])
            ld(mbv[:], mb_h[l][:])
            ld(l2wv[:], l2w_h[l][:])
            ld(l2bv[:], l2b_h[l][:])
            psD = [Bank(d_psF0), Bank(d_psF1)]
            NCH_D = S // CB
            dln_mark = [0]
            mpe_mark = [0]

            def d_top(j):
                par = j % 2
                c0 = j * CB
                sync.wait_ge(VEC.h, VEC.n)
                sync.wait_ge(ACTC.h, ACTC.n)
                sync.wait_ge(LD.h, LD.n)
                ld(dx2[par][:], xa_s[:, :, c0:c0 + CB].transpose([1, 0, 2]))
                for e in (act, ten):
                    e.wait_ge(LD.h, LD.n)
                    e.wait_ge(WL.h, WL.n)
                ln_stats_top(dx2[par][:], CB, dxb[:], dln2[par][:], d_psA, d_psB)

            def d_ln(j):
                par = j % 2
                psRb, psMb = ln_stats_fin(CB, d_psA, d_psB, d_psR, d_psM)
                gam = l2wv[:].unsqueeze(2).broadcast_to([128, F, CB])
                bet = l2bv[:].unsqueeze(2).broadcast_to([128, F, CB])
                vec.tensor_tensor(dgs[:], dx2[par][:], psRb, ALU.mult)
                vec.tensor_tensor(dgs[:], dgs[:], psMb, ALU.subtract)
                vec.tensor_tensor(dgs[:], dgs[:], gam, ALU.mult)
                vec.tensor_tensor(dln2[par][:], dgs[:], bet, ALU.add)
                dln_mark[0] = VEC.n

            def d_fc(i):
                par = i % 2
                mw_prefetch(i * F + 2)
                ten.wait_ge(VEC.h, dln_mark[0])
                for mo in range(32):
                    bk = psD[mo % 2]
                    bk.wait()
                    ii = None
                    for k in range(F):
                        ii = ten.matmul(bk.ps[:, :CB], fw_sb[:, k, mo * 128:(mo + 1) * 128],
                                        dln2[par][:, k, :], start=(k == 0), stop=(k == F - 1))
                    TEN.inc(ii)
                    act.wait_ge(TEN.h, TEN.n)
                    if mo == 0 and mpe_mark[0]:
                        act.wait_ge(TEN.h, mpe_mark[0])
                    act.activation(dg[:, mo, :], bk.ps[:, :CB], AF.Gelu,
                                   bias=fbv[:, mo:mo + 1])
                    bk.a = ACTC.n

            def d_mproj(i):
                par = i % 2
                c0 = i * CB
                ten.wait_ge(ACTC.h, ACTC.n)
                for m2 in range(F):
                    g = i * F + m2
                    mw_prefetch(g + 3)
                    slot = g % 3
                    bk = psD[m2 % 2]
                    bk.wait()
                    ten.wait_ge(WL.h, mw_tok[slot])
                    ii = None
                    for mo in range(32):
                        ii = ten.matmul(bk.ps[:, :CB], mwb[slot][:, mo, 0:128],
                                        dg[:, mo, :], start=(mo == 0), stop=(mo == 31))
                    mw_cons[slot] = TEN.inc(ii)
                    vec.wait_ge(TEN.h, TEN.n)
                    vec.scalar_tensor_tensor(
                        dx2[par][:, m2, :], bk.ps[:, :CB], mbv[:, m2:m2 + 1],
                        dx2[par][:, m2, :], ALU.add, ALU.add)
                    bk.v = VEC.n
                mpe_mark[0] = mw_cons[(i * F + F - 1) % 3]
                sync.wait_ge(VEC.h, VEC.n)
                ld(xa_s[:, :, c0:c0 + CB].transpose([1, 0, 2]), dx2[par][:])

            d_top(0)
            d_ln(0)
            for i in range(NCH_D):
                if i + 1 < NCH_D:
                    d_top(i + 1)
                d_fc(i)
                if i + 1 < NCH_D:
                    d_ln(i + 1)
                d_mproj(i)
        bar()
        if l == 0:
            tap3("xa_0", xa_s)
        if l == NL - 1:
            tap3("xa_3", xa_s)

    with contextlib.ExitStack() as stg, scope("R"):
        gbv = stg.enter_context(nc.sbuf_tensor(uname("gbv"), [NE, 1], F32))
        sfc = stg.enter_context(nc.sbuf_tensor(uname("sfc"), [NE, S], F32))
        sfcT = stg.enter_context(nc.sbuf_tensor(uname("sfcT"), [128, 16, NE], F32))
        r_gs = stg.enter_context(nc.sbuf_tensor(uname("r_gs"), [128, 16, 4], F32))
        r_t1 = stg.enter_context(nc.sbuf_tensor(uname("r_t1"), [128, 16, 4], F32))
        r_t2 = stg.enter_context(nc.sbuf_tensor(uname("r_t2"), [128, 16, 4], F32))
        r_r1 = stg.enter_context(nc.sbuf_tensor(uname("r_r1"), [128, 16], F32))
        r_r2 = stg.enter_context(nc.sbuf_tensor(uname("r_r2"), [128, 16], F32))
        r_ms = stg.enter_context(nc.sbuf_tensor(uname("r_ms"), [128, 16, NE], F32))
        r_tc = stg.enter_context(nc.sbuf_tensor(uname("r_tc"), [128, 16, NE], F32))
        r_cm = stg.enter_context(nc.sbuf_tensor(uname("r_cm"), [128, 16, NE], F32))
        r_ps = stg.enter_context(nc.psum_tensor(uname("r_ps"), [128, 16], F32))
        r_psT = stg.enter_context(nc.psum_tensor(uname("r_psT"), [16, 128], F32))
        stage_start()
        ld(gbv[:], gb_h[:])
        act.wait_ge(LD.h, LD.n)
        ACTC.inc(act.activation(sfc[:], logits[:], AF.Sigmoid))
        vec.wait_ge(ACTC.h, ACTC.n)
        VEC.inc(vec.tensor_scalar(sfc[:], sfc[:], gbv[:], None, ALU.add))
        ten.wait_ge(VEC.h, VEC.n)
        for t in range(16):
            if t > 0:
                ten.wait_ge(VEC.h, VEC.n)
            TEN.inc(ten.transpose(r_ps[:], sfc[:, t * 128:(t + 1) * 128], ident[0:16, 0:16]))
            vec.wait_ge(TEN.h, TEN.n)
            VEC.inc(vec.tensor_copy(sfcT[:, t, :], r_ps[:]))
        if "sfcT" in taps:
            sync.wait_ge(VEC.h, VEC.n)
            ld(taps["sfcT"][:], sfcT[:].rearrange("p t e -> p (t e)"))
        s4 = sfcT[:].rearrange("p t (g j) -> p t g j", g=4)
        vec.reduce_max(r_gs[:], s4, axis=AX.X)
        m1b = r_gs[:].unsqueeze(3).broadcast_to([128, 16, 4, 4])
        t4 = r_ms[:].rearrange("p t (g j) -> p t g j", g=4)
        vec.tensor_tensor(t4, s4, m1b, ALU.is_ge)
        vec.scalar_tensor_tensor(t4, t4, NEG, s4, ALU.mult, ALU.add)
        vec.reduce_max(r_t1[:], t4, axis=AX.X)
        vec.tensor_tensor(r_gs[:], r_gs[:], r_t1[:], ALU.add)
        vec.reduce_max(r_r1[:], r_gs[:], axis=AX.X)
        g1b = r_r1[:].unsqueeze(2).broadcast_to([128, 16, 4])
        vec.tensor_tensor(r_t1[:], r_gs[:], g1b, ALU.is_ge)
        vec.scalar_tensor_tensor(r_t1[:], r_t1[:], NEG, r_gs[:], ALU.mult, ALU.add)
        vec.reduce_max(r_r2[:], r_t1[:], axis=AX.X)
        r2b = r_r2[:].unsqueeze(2).broadcast_to([128, 16, 4])
        vec.tensor_tensor(r_t1[:], r_gs[:], r2b, ALU.is_ge)
        mb4 = r_t1[:].unsqueeze(3).broadcast_to([128, 16, 4, 4])
        vec.tensor_tensor(r_ms[:].rearrange("p t (g j) -> p t g j", g=4),
                          s4, mb4, ALU.mult)
        if "ms" in taps:
            sync.wait_ge(VEC.h, VEC.n)
            ld(taps["ms"][:], r_ms[:].rearrange("p t e -> p (t e)"))
        vec.tensor_copy(r_tc[:], r_ms[:])
        for it in range(4):
            vec.reduce_max(r_r1[:], r_tc[:], axis=AX.X)
            if it < 3:
                r1b = r_r1[:].unsqueeze(2).broadcast_to([128, 16, NE])
                vec.tensor_tensor(r_cm[:], r_tc[:], r1b, ALU.is_ge)
                vec.scalar_tensor_tensor(r_tc[:], r_cm[:], NEG, r_tc[:],
                                         ALU.mult, ALU.add)
        m4b = r_r1[:].unsqueeze(2).broadcast_to([128, 16, NE])
        vec.tensor_tensor(r_cm[:], r_ms[:], m4b, ALU.is_ge)
        vec.tensor_tensor(r_cm[:], r_cm[:], r_ms[:], ALU.mult)
        vec.reduce_sum(r_r1[:], r_cm[:], axis=AX.X)
        vec.tensor_scalar(r_r1[:], r_r1[:], 1e-20, None, ALU.add)
        vec.reciprocal(r_r2[:], r_r1[:])
        vec.tensor_scalar(r_r2[:], r_r2[:], 2.5, None, ALU.mult)
        wb = r_r2[:].unsqueeze(2).broadcast_to([128, 16, NE])
        vec.tensor_tensor(r_cm[:], r_cm[:], wb, ALU.mult)
        if "rcm" in taps:
            sync.wait_ge(VEC.h, VEC.n)
            ld(taps["rcm"][:], r_cm[:].rearrange("p t e -> p (t e)"))
            sync.wait_ge(LD.h, LD.n)
        ten.wait_ge(VEC.h, VEC.n)
        for t in range(16):
            if t > 0:
                ten.wait_ge(VEC.h, VEC.n)
            TEN.inc(ten.transpose(r_psT[:], r_cm[:, t, :], ident[:]))
            vec.wait_ge(TEN.h, TEN.n)
            VEC.inc(vec.tensor_copy(sfc[:, t * 128:(t + 1) * 128], r_psT[:]))
        sync.wait_ge(VEC.h, VEC.n)
        ld(comb_s[:], sfc[:, :])
    bar()
    if "comb" in taps:
        pool.wait_ge(LD.h, LD.n)
        wld(taps["comb"][:], comb_s[:])
        pool.wait_ge(WL.h, WL.n)
        bar()

    with contextlib.ExitStack() as stg, scope("SH"):
        swg_sb = stg.enter_context(nc.sbuf_tensor(uname("swg_sb"), [128, F, E], BF16))
        swu_sb = stg.enter_context(nc.sbuf_tensor(uname("swu_sb"), [128, F, E], BF16))
        swd_sb = stg.enter_context(nc.sbuf_tensor(uname("swd_sb"), [128, F, E], BF16))
        shx = stg.enter_context(nc.sbuf_tensor(uname("shx"), [128, F, CB], BF16))
        sha = stg.enter_context(nc.sbuf_tensor(uname("sha"), [128, F, CB], F32))
        shg = stg.enter_context(nc.sbuf_tensor(uname("shg"), [128, F, CB], BF16))
        shi = stg.enter_context(nc.sbuf_tensor(uname("shi"), [128, F, CB], BF16))
        sh_ps0 = stg.enter_context(nc.psum_tensor(uname("sh_ps0"), [128, CB], F32))
        sh_ps1 = stg.enter_context(nc.psum_tensor(uname("sh_ps1"), [128, CB], F32))
        stage_start()
        wld(swg_sb[:], swg_h.rearrange("(k p) m -> p k m", p=128))
        wld(swu_sb[:], swu_h.rearrange("(k p) m -> p k m", p=128))
        wld(swd_sb[:], swd_h.rearrange("(k p) m -> p k m", p=128))
        psS = [Bank(sh_ps0), Bank(sh_ps1)]
        for ci in range(S // CB):
            c0 = ci * CB
            sync.wait_ge(VEC.h, VEC.n)
            sync.wait_ge(ACTC.h, ACTC.n)
            sync.wait_ge(LD.h, LD.n)
            ld(shx[:], xnb_s[:, :, c0:c0 + CB].transpose([1, 0, 2]))
            ld(sha[:], xa_s[:, :, c0:c0 + CB].transpose([1, 0, 2]))
            for e in (act, ten, vec):
                e.wait_ge(LD.h, LD.n)
                e.wait_ge(WL.h, WL.n)
            for m in range(F):
                bk = psS[m % 2]
                bk.wait()
                i = None
                for k in range(F):
                    i = ten.matmul(bk.ps[:, :CB], swg_sb[:, k, m * 128:(m + 1) * 128],
                                   shx[:, k, :], start=(k == 0), stop=(k == F - 1))
                TEN.inc(i)
                act.wait_ge(TEN.h, TEN.n)
                bk.a = ACTC.inc(act.activation(shg[:, m, :], bk.ps[:, :CB], AF.Silu))
            ten.wait_ge(ACTC.h, ACTC.n)
            for m in range(F):
                bk = psS[m % 2]
                bk.wait()
                i = None
                for k in range(F):
                    i = ten.matmul(bk.ps[:, :CB], swu_sb[:, k, m * 128:(m + 1) * 128],
                                   shx[:, k, :], start=(k == 0), stop=(k == F - 1))
                TEN.inc(i)
                vec.wait_ge(TEN.h, TEN.n)
                bk.v = VEC.inc(vec.tensor_tensor(
                    shi[:, m, :], bk.ps[:, :CB], shg[:, m, :], ALU.mult))
            ten.wait_ge(VEC.h, VEC.n)
            for m in range(F):
                bk = psS[m % 2]
                bk.wait()
                i = None
                for k in range(F):
                    i = ten.matmul(bk.ps[:, :CB], swd_sb[:, k, m * 128:(m + 1) * 128],
                                   shi[:, k, :], start=(k == 0), stop=(k == F - 1))
                TEN.inc(i)
                vec.wait_ge(TEN.h, TEN.n)
                bk.v = VEC.inc(vec.tensor_tensor(
                    sha[:, m, :], bk.ps[:, :CB], sha[:, m, :], ALU.add))
            sync.wait_ge(VEC.h, VEC.n)
            ld(xa_s[:, :, c0:c0 + CB].transpose([1, 0, 2]), sha[:])
    bar()

    with contextlib.ExitStack() as stg, scope("EX"):
        ewr = [stg.enter_context(nc.sbuf_tensor(uname("ewr"), [128, 2, F, 512], BF16))
               for _ in range(2)]
        eur = [stg.enter_context(nc.sbuf_tensor(uname("eur"), [128, 2, F, 512], BF16))
               for _ in range(2)]
        edr = [stg.enter_context(nc.sbuf_tensor(uname("edr"), [128, 2, 4, E], BF16))
               for _ in range(2)]
        ece = stg.enter_context(nc.sbuf_tensor(uname("ece"), [128, 2, CB], F32))
        exn = stg.enter_context(nc.sbuf_tensor(uname("exn"), [128, F, CB], BF16))
        exa = stg.enter_context(nc.sbuf_tensor(uname("exa"), [128, F, CB], F32))
        esg = stg.enter_context(nc.sbuf_tensor(uname("esg"), [128, 4, CB], BF16))
        eic = stg.enter_context(nc.sbuf_tensor(uname("eic"), [128, 2, 4, CB], BF16))
        e_ps0 = stg.enter_context(nc.psum_tensor(uname("e_ps0"), [128, CB], F32))
        e_ps1 = stg.enter_context(nc.psum_tensor(uname("e_ps1"), [128, CB], F32))
        stage_start()
        psE = [Bank(e_ps0), Bank(e_ps1)]
        ring_cons = [0, 0]
        ring_tok = [0, 0]

        def ex_load(s):
            slot = s % 2
            pool.wait_ge(TEN.h, ring_cons[slot])
            for j in range(2):
                e_id = s * 2 + j
                wld(ewr[slot][:, j, :, :], wg_h[e_id].rearrange("(k p) m -> p k m", p=128))
                wld(eur[slot][:, j, :, :], wu_h[e_id].rearrange("(k p) m -> p k m", p=128))
                wld(edr[slot][:, j, :, :], wd_h[e_id].rearrange("(k p) m -> p k m", p=128))
            ring_tok[slot] = WL.n

        ex_load(0)
        for s in range(8):
            slot = s % 2
            if s < 7:
                ex_load(s + 1)
            last_mm = 0
            for ci in range(S // CB):
                c0 = ci * CB
                sync.wait_ge(VEC.h, VEC.n)
                sync.wait_ge(ACTC.h, ACTC.n)
                sync.wait_ge(LD.h, LD.n)
                ld(exn[:], xnb_s[:, :, c0:c0 + CB].transpose([1, 0, 2]))
                ld(exa[:], xa_s[:, :, c0:c0 + CB].transpose([1, 0, 2]))
                for j in range(2):
                    e_id = s * 2 + j
                    ld(ece[:, j, :], comb_s[e_id:e_id + 1, c0:c0 + CB]
                       .broadcast_to([128, CB]))
                for e in (act, ten, vec):
                    e.wait_ge(LD.h, LD.n)
                ten.wait_ge(WL.h, ring_tok[slot])
                for j in range(2):
                    for m in range(4):
                        bk = psE[m % 2]
                        bk.wait()
                        i = None
                        for k in range(F):
                            i = ten.matmul(bk.ps[:, :CB],
                                           ewr[slot][:, j, k, m * 128:(m + 1) * 128],
                                           exn[:, k, :], start=(k == 0),
                                           stop=(k == F - 1))
                        TEN.inc(i)
                        act.wait_ge(TEN.h, TEN.n)
                        act.activation(esg[:, m, :], bk.ps[:, :CB], AF.Silu)
                        bk.a = ACTC.n
                    for m in range(4):
                        bk = psE[m % 2]
                        bk.wait()
                        i = None
                        for k in range(F):
                            i = ten.matmul(bk.ps[:, :CB],
                                           eur[slot][:, j, k, m * 128:(m + 1) * 128],
                                           exn[:, k, :], start=(k == 0),
                                           stop=(k == F - 1))
                        TEN.inc(i)
                        vec.wait_ge(TEN.h, TEN.n)
                        vec.tensor_tensor(eic[:, j, m, :], bk.ps[:, :CB],
                                          esg[:, m, :], ALU.mult)
                        vec.tensor_tensor(eic[:, j, m, :], eic[:, j, m, :],
                                          ece[:, j, :], ALU.mult)
                        bk.v = VEC.n
                    act.wait_ge(VEC.h, VEC.n)
                ten.wait_ge(VEC.h, VEC.n)
                for m2 in range(F):
                    bk = psE[m2 % 2]
                    bk.wait()
                    i = None
                    for j in range(2):
                        for k in range(4):
                            i = ten.matmul(bk.ps[:, :CB],
                                           edr[slot][:, j, k, m2 * 128:(m2 + 1) * 128],
                                           eic[:, j, k, :],
                                           start=(j == 0 and k == 0),
                                           stop=(j == 1 and k == 3))
                    last_mm = TEN.inc(i)
                    vec.wait_ge(TEN.h, TEN.n)
                    vec.tensor_tensor(exa[:, m2, :], bk.ps[:, :CB],
                                      exa[:, m2, :], ALU.add)
                    bk.v = VEC.n
                sync.wait_ge(VEC.h, VEC.n)
                ld(xa_s[:, :, c0:c0 + CB].transpose([1, 0, 2]), exa[:])
            ring_cons[slot] = last_mm
    bar()
    tap3("moe", xa_s)

    with contextlib.ExitStack() as stg, scope("FIN"):
        fxa = stg.enter_context(nc.sbuf_tensor(uname("fxa"), [128, F, 128], F32))
        fx = stg.enter_context(nc.sbuf_tensor(uname("fx"), [128, F, 128], F32))
        frow = stg.enter_context(nc.sbuf_tensor(uname("frow"), [128, E], F32))
        f_ps = stg.enter_context(nc.psum_tensor(uname("f_ps"), [128, 128], F32))
        stage_start()
        for t in range(S // 128):
            sync.wait_ge(VEC.h, VEC.n)
            ld(fxa[:], xa_s[:, :, t * 128:(t + 1) * 128].transpose([1, 0, 2]))
            ld(fx[:], xT_s[:, :, t * 128:(t + 1) * 128].transpose([1, 0, 2]))
            vec.wait_ge(LD.h, LD.n)
            VEC.inc(vec.tensor_tensor(fxa[:], fxa[:], fx[:], ALU.add))
            ten.wait_ge(VEC.h, VEC.n)
            for f in range(F):
                if f > 0:
                    ten.wait_ge(VEC.h, VEC.n)
                TEN.inc(ten.transpose(f_ps[:], fxa[:, f, :], ident[:]))
                vec.wait_ge(TEN.h, TEN.n)
                VEC.inc(vec.tensor_copy(frow[:, f * 128:(f + 1) * 128], f_ps[:]))
            sync.wait_ge(VEC.h, VEC.n)
            ld(out_h[t * 128:(t + 1) * 128, :], frow[:])
        sync.wait_ge(LD.h, LD.n)
    bar()

    stack.close()
    return nc


_PROGRAM_CACHE = {}


def kernel(**inputs):
    consts = _build_constants()
    perm = _perm_qkv()

    x = np.ascontiguousarray(np.asarray(inputs["L0_attn_x"], np.float32))
    aw = np.asarray(inputs["L0_attn_attn_w"], np.float32)
    ab = np.asarray(inputs["L0_attn_attn_b"], np.float32)
    base = {}
    for l in range(NL):
        base[f"aw{l}"] = np.ascontiguousarray(aw[l][:, perm])
        base[f"ab{l}"] = _cols(ab[l][perm], 24)
        base[f"pw{l}"] = np.ascontiguousarray(np.asarray(inputs["L0_attn_aproj_w"][l], np.float32))
        base[f"pb{l}"] = _cols(inputs["L0_attn_aproj_b"][l], 8)
        base[f"fw{l}"] = np.ascontiguousarray(np.asarray(inputs["L0_attn_fc_w"][l], np.float32))
        base[f"fb{l}"] = _cols(inputs["L0_attn_fc_b"][l], 32)
        base[f"mw{l}"] = np.ascontiguousarray(np.asarray(inputs["L0_attn_mproj_w"][l], np.float32))
        base[f"mb{l}"] = _cols(inputs["L0_attn_mproj_b"][l], 8)
        base[f"l1w{l}"] = _cols(inputs["L0_attn_ln1_w"][l], 8)
        base[f"l1b{l}"] = _cols(inputs["L0_attn_ln1_b"][l], 8)
        base[f"l2w{l}"] = _cols(inputs["L0_attn_ln2_w"][l], 8)
        base[f"l2b{l}"] = _cols(inputs["L0_attn_ln2_b"][l], 8)
    base["gk"] = np.ascontiguousarray(np.asarray(inputs["L0_ffn_gate_kernel"], np.float32))
    base["gb"] = np.ascontiguousarray(np.asarray(inputs["L0_ffn_gate_bias"], np.float32).reshape(NE, 1))
    base["wg"] = np.ascontiguousarray(np.asarray(inputs["L0_ffn_w_gate"], np.float32))
    base["wu"] = np.ascontiguousarray(np.asarray(inputs["L0_ffn_w_up"], np.float32))
    base["wd"] = np.ascontiguousarray(np.asarray(inputs["L0_ffn_w_down"], np.float32))
    base["swg"] = np.ascontiguousarray(np.asarray(inputs["L0_ffn_sw_gate"], np.float32))
    base["swu"] = np.ascontiguousarray(np.asarray(inputs["L0_ffn_sw_up"], np.float32))
    base["swd"] = np.ascontiguousarray(np.asarray(inputs["L0_ffn_sw_down"], np.float32))
    base.update(consts)

    nc = build_program()
    in_maps = []
    for c in range(8):
        m = dict(base)
        m["x"] = np.ascontiguousarray(x[c])
        in_maps.append(m)
    res = run_bass_kernel_spmd(nc, in_maps, core_ids=list(range(8)))
    out = np.stack([np.asarray(r["out"], np.float32) for r in res.results], axis=0)
    if DEBUG_TAPS:
        kernel._taps = [{k: np.asarray(v) for k, v in r.items() if k.startswith("tap_")}
                        for r in res.results]
    return out


# version 21
# speedup vs baseline: 1.0373x; 1.0373x over previous
import numpy as np
import concourse.bass as bass
import concourse.mybir as mybir
from concourse.bass_utils import run_bass_kernel_spmd

F32 = mybir.dt.float32
BF16 = mybir.dt.bfloat16
AF = mybir.ActivationFunctionType
ALU = mybir.AluOpType
AX = mybir.AxisListType

S = 2048
E = 1024
F = 8
H, HD = 16, 64
NL = 4
NE = 16
EPS = 1e-5
CA = 256
CB = 512
NEG = -1.0e30

DEBUG_TAPS = []
SIM_STRICT = False


def _build_constants():
    import ml_dtypes
    ks = np.fft.fftfreq(S)
    kh = np.fft.fftfreq(H)
    kd = np.fft.fftfreq(HD)
    km = np.sqrt(ks[:, None, None] ** 2 + kh[None, :, None] ** 2
                 + kd[None, None, :] ** 2 + 1e-10)
    c = np.cos(np.arctan(np.log(km + 1e-10)))
    hh = np.arange(H)[:, None]
    dd = np.arange(HD)[None, :]
    cols, eigs, seen = [], [], set()
    for fh in range(H):
        for fd in range(HD):
            if (fh, fd) in seen:
                continue
            ph, pd = (-fh) % H, (-fd) % HD
            seen.add((fh, fd)); seen.add((ph, pd))
            phase = 2 * np.pi * (fh * hh / H + fd * dd / HD)
            if (ph, pd) == (fh, fd):
                cols.append((np.cos(phase) / np.sqrt(E)).reshape(-1))
                eigs.append(c[:, fh, fd])
            else:
                cols.append((np.cos(phase) * np.sqrt(2.0 / E)).reshape(-1))
                eigs.append(c[:, fh, fd])
                cols.append((np.sin(phase) * np.sqrt(2.0 / E)).reshape(-1))
                eigs.append(c[:, fh, fd])
    Q = np.stack(cols, axis=1)
    ctil = np.stack(eigs, axis=0)
    lam = np.arange(S) / S
    t = np.arange(S) * 0.1
    mod = 1.0 + 0.1 * (np.sin(t + lam) * np.cos(t - 2.0 * lam + 0.5 * lam * lam))
    return {
        "cQ": np.ascontiguousarray(Q.astype(np.float32).astype(ml_dtypes.bfloat16)),
        "cCT": np.ascontiguousarray(ctil.astype(np.float32)),
        "cMOD": np.ascontiguousarray(np.broadcast_to(
            mod.astype(np.float32)[None, :], (128, S))),
        "cID": np.eye(128, dtype=np.float32),
        "cONEB": np.ones((128, 1), dtype=ml_dtypes.bfloat16),
        "cONER": np.ones((1, 128), dtype=np.float32),
    }


def _perm_qkv():
    idx = np.arange(3 * E).reshape(H, 3, HD)
    return np.concatenate([idx[:, 0, :].reshape(-1),
                           idx[:, 1, :].reshape(-1),
                           idx[:, 2, :].reshape(-1)])


def _cols(v, ntiles):
    return np.ascontiguousarray(np.asarray(v, np.float32).reshape(ntiles, 128).T)


class Ctr:
    def __init__(self, nc, name, step):
        self.h = nc.alloc_semaphore(name)
        self.n = 0
        self.step = step
        self.last = None

    def inc(self, instr):
        if instr is self.last:
            return self.n
        instr.then_inc(self.h, self.step)
        self.n += self.step
        self.last = instr
        return self.n


_AUTO_OPS = {
    "reciprocal",
    "tensor_tensor", "tensor_scalar", "tensor_single_scalar", "tensor_copy",
    "tensor_add", "tensor_sub", "tensor_mul", "tensor_max", "memset", "memzero",
    "reciprocal", "reduce_max", "reduce_sum", "tensor_reduce",
    "scalar_tensor_tensor", "select", "activation", "copy", "mul", "add",
    "square", "sqrt", "sign",
}


class MarkedEng:
    def __init__(self, eng, ctr):
        self._e = eng
        self._c = ctr

    def __getattr__(self, name):
        f = getattr(self._e, name)
        if name in _AUTO_OPS:
            c = self._c
            eng = self._e
            always = name == "reciprocal"

            def g(*a, **k):
                i = f(*a, **k)
                c.inc(i)
                small = SIM_STRICT
                if not small:
                    try:
                        small = a[0].free_size() < 256 or a[0].partition_size() == 1
                    except Exception:
                        small = True
                if always or small:
                    eng.wait_ge(c.h, c.n)
                return i
            return g
        return f


def build_program():
    nc = bass.Bass("TRN2", target_bir_lowering=False)
    sync, ten, vec, act, pool = nc.sync, nc.tensor, nc.vector, nc.scalar, nc.gpsimd

    def din(name, shape, dtype=F32):
        return nc.dram_tensor(name, list(shape), dtype, kind="ExternalInput").ap()

    x_h = din("x", (S, E))
    aw_h = [din(f"aw{l}", (E, 3 * E)) for l in range(NL)]
    ab_h = [din(f"ab{l}", (128, 24)) for l in range(NL)]
    pw_h = [din(f"pw{l}", (E, E)) for l in range(NL)]
    pb_h = [din(f"pb{l}", (128, 8)) for l in range(NL)]
    fw_h = [din(f"fw{l}", (E, 4 * E)) for l in range(NL)]
    fb_h = [din(f"fb{l}", (128, 32)) for l in range(NL)]
    mw_h = [din(f"mw{l}", (4 * E, E)) for l in range(NL)]
    mb_h = [din(f"mb{l}", (128, 8)) for l in range(NL)]
    l1w_h = [din(f"l1w{l}", (128, 8)) for l in range(NL)]
    l1b_h = [din(f"l1b{l}", (128, 8)) for l in range(NL)]
    l2w_h = [din(f"l2w{l}", (128, 8)) for l in range(NL)]
    l2b_h = [din(f"l2b{l}", (128, 8)) for l in range(NL)]
    gk_h = din("gk", (E, NE))
    gb_h = din("gb", (NE, 1))
    wg_h = din("wg", (NE, E, 512))
    wu_h = din("wu", (NE, E, 512))
    wd_h = din("wd", (NE, 512, E))
    swg_h = din("swg", (E, E))
    swu_h = din("swu", (E, E))
    swd_h = din("swd", (E, E))
    cQ_h = din("cQ", (E, E), BF16)
    cCT_h = din("cCT", (E, S))
    cMOD_h = din("cMOD", (128, S))
    cID_h = din("cID", (128, 128))
    cONEB_h = din("cONEB", (128, 1), BF16)
    cONER_h = din("cONER", (1, 128))

    out_h = nc.dram_tensor("out", [S, E], F32, kind="ExternalOutput").ap()

    def scr(name, shape, dtype=F32):
        return nc.dram_tensor(name, list(shape), dtype, kind="Internal").ap()

    xT_s = scr("xT_s", (F, 128, S))
    xa_s = scr("xa_s", (F, 128, S))
    xnb_s = scr("xnb_s", (F, 128, S), BF16)
    p0_s = scr("p0_s", (F, 128, S), BF16)
    comb_s = scr("comb_s", (NE, S))
    ln1_s = scr("ln1_s", (F, 128, S), BF16) if "ln1_0" in DEBUG_TAPS else None
    qkv_s = scr("qkv_s", (24, 128, S)) if "qkv_0" in DEBUG_TAPS else None

    taps = {}
    _tapshape = {"comb": [NE, S], "qkv_0": [24, 128, S], "sfcT": [128, 256],
                 "ms": [128, 256], "rcm": [128, 256]}
    for t in DEBUG_TAPS:
        shp = _tapshape.get(t, [F, 128, S])
        taps[t] = nc.dram_tensor("tap_" + t, shp, F32, kind="ExternalOutput").ap()

    LD = Ctr(nc, "cLD", 16)
    WL = Ctr(nc, "cWL", 16)
    TEN = Ctr(nc, "cTEN", 1)
    VEC = Ctr(nc, "cVEC", 1)
    ACTC = Ctr(nc, "cACT", 1)
    POOLC = Ctr(nc, "cPOOL", 1)
    vec = MarkedEng(vec, VEC)
    act = MarkedEng(act, ACTC)
    plc = MarkedEng(pool, POOLC)

    def ld(dst, src):
        if SIM_STRICT:
            sync.wait_ge(LD.h, LD.n)
        sync.dma_start(dst, src).then_inc(LD.h, 16)
        LD.n += 16

    def wld(dst, src):
        if SIM_STRICT:
            pool.wait_ge(WL.h, WL.n)
        pool.dma_start(dst, src).then_inc(WL.h, 16)
        WL.n += 16

    def bar():
        nc.all_engine_barrier()

    import contextlib as _ctx

    @_ctx.contextmanager
    def scope(nm):
        with nc.named_scope(nm, notify=True):
            yield

    def stage_start():
        for e in (sync, pool, ten, nc.vector, nc.scalar):
            e.wait_ge(LD.h, LD.n)
            e.wait_ge(WL.h, WL.n)

    class Bank:
        __slots__ = ("ps", "a", "v")
        def __init__(self, ps):
            self.ps = ps
            self.a = 0
            self.v = 0

        def wait(self):
            if self.a:
                ten.wait_ge(ACTC.h, self.a)
            if self.v:
                ten.wait_ge(VEC.h, self.v)

    def tap3(name, src_scr):
        if name in taps:
            bar()
            pool.wait_ge(LD.h, LD.n)
            wld(taps[name][:], src_scr[:])
            pool.wait_ge(WL.h, WL.n)
            bar()

    import contextlib
    stack = contextlib.ExitStack()
    _names = {}

    def uname(n):
        _names[n] = _names.get(n, -1) + 1
        return f"{n}_{_names[n]}" if _names[n] else n

    def sb(name, shape, dtype=F32):
        return stack.enter_context(nc.sbuf_tensor(name, list(shape), dtype))

    ident = sb("ident", (128, 128))
    oneb = sb("oneb", (128, 1), BF16)
    oner = sb("oner", (1, 128))
    logits = sb("logits", (NE, S))
    rows_mu = sb("rows_mu", (1, 520))
    rows_rs = sb("rows_rs", (1, 520))
    rows_tp = sb("rows_tp", (1, 520))

    epsv = sb("epsv", (128, 1))
    vec.memset(epsv[:], float(EPS))
    VEC.inc(vec.engine_nop()) if False else None
    ld(ident[:], cID_h[:])
    ld(oneb[:], cONEB_h[:])
    ld(oner[:], cONER_h[:])

    with contextlib.ExitStack() as stg, scope("IN"):
        in_rows = stg.enter_context(nc.sbuf_tensor(uname("in_rows"), [128, E], F32))
        in_cols = stg.enter_context(nc.sbuf_tensor(uname("in_cols"), [128, F, 128], F32))
        in_ps = stg.enter_context(nc.psum_tensor(uname("in_ps"), [128, 128], F32))
        for t in range(S // 128):
            sync.wait_ge(VEC.h, VEC.n)
            ld(in_rows[:], x_h[t * 128:(t + 1) * 128, :])
            ten.wait_ge(LD.h, LD.n)
            for f in range(F):
                if f > 0:
                    ten.wait_ge(VEC.h, VEC.n)
                TEN.inc(ten.transpose(in_ps[:], in_rows[:, f * 128:(f + 1) * 128],
                                      ident[:]))
                vec.wait_ge(TEN.h, TEN.n)
                VEC.inc(vec.tensor_copy(in_cols[:, f, :], in_ps[:]))
            sync.wait_ge(VEC.h, VEC.n)
            ld(xT_s[:, :, t * 128:(t + 1) * 128].transpose([1, 0, 2]), in_cols[:])
    bar()

    def ln_stats_top(src3, C, xb, xq, psA, psB):
        act.wait_ge(VEC.h, VEC.n)
        act.activation(xb, src3, AF.Copy)
        act.activation(xq, src3, AF.Square)
        ten.wait_ge(ACTC.h, ACTC.n)
        ten.wait_ge(VEC.h, VEC.n)
        for f in range(F):
            ten.matmul(psA[:, :C], oneb[:], xb[:, f, :], start=(f == 0), stop=(f == F - 1))
        i = None
        for f in range(F):
            i = ten.matmul(psB[:, :C], oneb[:], xq[:, f, :], start=(f == 0), stop=(f == F - 1))
        TEN.inc(i)

    def ln_stats_fin(C, psA, psB, psR, psM):
        mu = rows_mu[:, 0:C]
        rs = rows_rs[:, 0:C]
        tp = rows_tp[:, 0:C]
        vec.wait_ge(TEN.h, TEN.n)
        vec.tensor_scalar(mu, psA[:, :C], 1.0 / E, None, ALU.mult)
        vec.tensor_scalar(tp, psB[:, :C], 1.0 / E, None, ALU.mult)
        vec.tensor_tensor(rs, mu, mu, ALU.mult)
        vec.tensor_tensor(tp, tp, rs, ALU.subtract)
        act.wait_ge(VEC.h, VEC.n)
        act.activation(tp, tp, AF.Ln, bias=epsv[0:1, 0:1])
        act.activation(rs, tp, AF.Exp, scale=-0.5)
        vec.wait_ge(ACTC.h, ACTC.n)
        vec.tensor_tensor(mu, mu, rs, ALU.mult)
        ten.wait_ge(VEC.h, VEC.n)
        ten.matmul(psR[:, :C], oner[:], rs, start=True, stop=True)
        TEN.inc(ten.matmul(psM[:, :C], oner[:], mu, start=True, stop=True))
        vec.wait_ge(TEN.h, TEN.n)
        psRb = psR[:, :C].unsqueeze(1).broadcast_to([128, F, C])
        psMb = psM[:, :C].unsqueeze(1).broadcast_to([128, F, C])
        return psRb, psMb

    def ln_stats(src3, C, xb, xq, psA, psB, psR, psM):
        ln_stats_top(src3, C, xb, xq, psA, psB)
        return ln_stats_fin(C, psA, psB, psR, psM)

    with contextlib.ExitStack() as stg, scope("OLN"):
        olx = stg.enter_context(nc.sbuf_tensor(uname("olx"), [128, F, CB], F32))
        olo = stg.enter_context(nc.sbuf_tensor(uname("olo"), [128, F, CB], F32))
        olob = stg.enter_context(nc.sbuf_tensor(uname("olob"), [128, F, CB], BF16))
        olxb = stg.enter_context(nc.sbuf_tensor(uname("olxb"), [128, F, CB], BF16))
        olxq = stg.enter_context(nc.sbuf_tensor(uname("olxq"), [128, F, CB], BF16))
        gksb = stg.enter_context(nc.sbuf_tensor(uname("gksb"), [128, F, NE], F32))
        ol_psA = stg.enter_context(nc.psum_tensor(uname("ol_psA"), [1, CB], F32))
        ol_psB = stg.enter_context(nc.psum_tensor(uname("ol_psB"), [1, CB], F32))
        ol_psR = stg.enter_context(nc.psum_tensor(uname("ol_psR"), [128, CB], F32))
        ol_psM = stg.enter_context(nc.psum_tensor(uname("ol_psM"), [128, CB], F32))
        ol_psG = stg.enter_context(nc.psum_tensor(uname("ol_psG"), [NE, CB], F32))
        stage_start()
        ld(gksb[:], gk_h.rearrange("(k p) m -> p k m", p=128))
        for ci in range(S // CB):
            c0 = ci * CB
            sync.wait_ge(VEC.h, VEC.n)
            sync.wait_ge(ACTC.h, ACTC.n)
            ld(olx[:], xT_s[:, :, c0:c0 + CB].transpose([1, 0, 2]))
            for e in (act, ten, vec):
                e.wait_ge(LD.h, LD.n)
            psRb, psMb = ln_stats(olx[:], CB, olxb[:], olxq[:],
                                  ol_psA, ol_psB, ol_psR, ol_psM)
            vec.tensor_tensor(olo[:], olx[:], psRb, ALU.mult)
            VEC.inc(vec.tensor_tensor(olo[:], olo[:], psMb, ALU.subtract))
            act.wait_ge(VEC.h, VEC.n)
            ACTC.inc(act.activation(olob[:], olo[:], AF.Copy))
            ten.wait_ge(VEC.h, VEC.n)
            i = None
            for f in range(F):
                i = ten.matmul(ol_psG[:, :CB], gksb[:, f, :], olo[:, f, :],
                               start=(f == 0), stop=(f == F - 1))
            TEN.inc(i)
            vec.wait_ge(TEN.h, TEN.n)
            VEC.inc(vec.tensor_copy(logits[:, c0:c0 + CB], ol_psG[:, :CB]))
            sync.wait_ge(VEC.h, VEC.n)
            sync.wait_ge(ACTC.h, ACTC.n)
            ld(xa_s[:, :, c0:c0 + CB].transpose([1, 0, 2]), olo[:])
            ld(xnb_s[:, :, c0:c0 + CB].transpose([1, 0, 2]), olob[:])
    bar()
    tap3("xn", xa_s)

    for l in range(NL):
        CAs = [128] * 16
        C0s = [sum(CAs[:j]) for j in range(len(CAs))]
        NCH = len(CAs)
        WMAX = 129
        with contextlib.ExitStack() as stg, scope(f"A{l}"):
            aw_sb = stg.enter_context(nc.sbuf_tensor(uname("aw_sb"), [128, F, 3 * E], BF16))
            abv = stg.enter_context(nc.sbuf_tensor(uname("abv"), [128, 24], F32))
            l1wv = stg.enter_context(nc.sbuf_tensor(uname("l1wv"), [128, 8], F32))
            l1bv = stg.enter_context(nc.sbuf_tensor(uname("l1bv"), [128, 8], F32))
            ax = stg.enter_context(nc.sbuf_tensor(uname("ax"), [128, F, WMAX], F32))
            axb = stg.enter_context(nc.sbuf_tensor(uname("axb"), [128, F, WMAX], BF16))
            axq = stg.enter_context(nc.sbuf_tensor(uname("axq"), [128, F, WMAX], BF16))
            lnb = stg.enter_context(nc.sbuf_tensor(uname("lnb"), [128, F, WMAX], BF16))
            qkv2 = [stg.enter_context(nc.sbuf_tensor(uname("qkvA"), [128, 24, WMAX], F32)),
                    stg.enter_context(nc.sbuf_tensor(uname("qkvB"), [128, 24, WMAX], F32))]
            p0b2 = [stg.enter_context(nc.sbuf_tensor(uname("p0bA"), [128, F, 128], BF16)),
                    stg.enter_context(nc.sbuf_tensor(uname("p0bB"), [128, F, 128], BF16))]
            def pair(nm):
                return [stg.enter_context(nc.sbuf_tensor(uname(nm), [128, F, 128], F32))
                        for _ in range(2)]
            tq3_2 = pair("tq3")
            tk3_2 = pair("tk3")
            tv3_2 = pair("tv3")
            trlq_2 = pair("trlq")
            trlk_2 = pair("trlk")
            trlv_2 = pair("trlv")
            tq1 = stg.enter_context(nc.sbuf_tensor(uname("tq1"), [128, F, 128], F32))
            tk1 = stg.enter_context(nc.sbuf_tensor(uname("tk1"), [128, F, 128], F32))
            tm = stg.enter_context(nc.sbuf_tensor(uname("tm"), [128, F, 128], F32))
            tt = stg.enter_context(nc.sbuf_tensor(uname("tt"), [128, F, 128], F32))
            att = stg.enter_context(nc.sbuf_tensor(uname("att"), [128, F, 128], F32))
            qk0 = stg.enter_context(nc.sbuf_tensor(uname("qk0"), [128, F, 128], F32))
            qk1 = stg.enter_context(nc.sbuf_tensor(uname("qk1"), [128, F, 128], F32))
            qk2 = stg.enter_context(nc.sbuf_tensor(uname("qk2"), [128, F, 128], F32))
            qk3 = stg.enter_context(nc.sbuf_tensor(uname("qk3"), [128, F, 128], F32))
            a_psA = stg.enter_context(nc.psum_tensor(uname("a_psA"), [1, WMAX], F32))
            a_psB = stg.enter_context(nc.psum_tensor(uname("a_psB"), [1, WMAX], F32))
            a_psR = stg.enter_context(nc.psum_tensor(uname("a_psR"), [128, WMAX], F32))
            a_psM = stg.enter_context(nc.psum_tensor(uname("a_psM"), [128, WMAX], F32))
            a_psQ = [stg.enter_context(nc.psum_tensor(uname("a_psQ"), [128, WMAX], F32))
                     for _ in range(4)]
            stage_start()
            wld(aw_sb[:], aw_h[l].rearrange("(k p) m -> p k m", p=128))
            ld(abv[:], ab_h[l][:])
            ld(l1wv[:], l1w_h[l][:])
            ld(l1bv[:], l1b_h[l][:])
            psQ = [Bank(p) for p in a_psQ]
            evac_marks = [0, 0]
            prep_marks = [0, 0]
            cos_marks = [0, 0]
            p0_mark = [0, 0]
            lnb_mark = [0]

            def front_top(j):
                Cj = CAs[j]
                c0 = C0s[j]
                hl = 1 if j > 0 else 0
                W = Cj + hl
                sync.wait_ge(VEC.h, VEC.n)
                sync.wait_ge(ACTC.h, ACTC.n)
                ld(ax[:, :, :W], xa_s[:, :, c0 - hl:c0 + Cj].transpose([1, 0, 2]))
                for e in (act, ten):
                    e.wait_ge(LD.h, LD.n)
                    e.wait_ge(WL.h, WL.n)
                ln_stats_top(ax[:, :, :W], W, axb[:, :, :W], axq[:, :, :W],
                             a_psA, a_psB)

            def front_dve(j):
                Cj = CAs[j]
                hl = 1 if j > 0 else 0
                W = Cj + hl
                psRb, psMb = ln_stats_fin(W, a_psA, a_psB, a_psR, a_psM)
                gam = l1wv[:].unsqueeze(2).broadcast_to([128, F, W])
                bet = l1bv[:].unsqueeze(2).broadcast_to([128, F, W])
                vec.tensor_tensor(ax[:, :, :W], ax[:, :, :W], psRb, ALU.mult)
                vec.tensor_tensor(ax[:, :, :W], ax[:, :, :W], psMb, ALU.subtract)
                vec.tensor_tensor(ax[:, :, :W], ax[:, :, :W], gam, ALU.mult)
                vec.tensor_tensor(lnb[:, :, :W], ax[:, :, :W], bet, ALU.add)
                lnb_mark[0] = VEC.n

            def front_mm(j):
                Cj = CAs[j]
                c0 = C0s[j]
                hl = 1 if j > 0 else 0
                W = Cj + hl
                par = j % 2
                qkv = qkv2[par]
                ten.wait_ge(VEC.h, lnb_mark[0])
                for m in range(24):
                    bk = psQ[m % 4]
                    bk.wait()
                    i = None
                    for k in range(F):
                        i = ten.matmul(bk.ps[:, :W], aw_sb[:, k, m * 128:(m + 1) * 128],
                                       lnb[:, k, :W], start=(k == 0), stop=(k == F - 1))
                    TEN.inc(i)
                    act.wait_ge(TEN.h, TEN.n)
                    if m == 0 and p0_mark[par]:
                        act.wait_ge(VEC.h, p0_mark[par])
                    act.activation(qkv[:, m, :W], bk.ps[:, :W], AF.Identity,
                                   bias=abv[:, m:m + 1])
                    bk.a = ACTC.n
                evac_marks[par] = ACTC.n
                q0 = qkv[:, 0:8, hl:W]
                k0 = qkv[:, 8:16, hl:W]
                v0 = qkv[:, 16:24, hl:W]

                def act_cos(dst, src_):
                    act.activation(att[:, :, :Cj], src_, AF.Sin, scale=0.5)
                    act.activation(att[:, :, :Cj], att[:, :, :Cj], AF.Square)
                    act.activation(dst[:, :, :Cj], att[:, :, :Cj], AF.Identity,
                                   scale=-2.0, bias=1.0)

                def act_roll(dst, zb):
                    act.copy(dst[64:128, :, :Cj], zb[0:64, :, :])
                    act.copy(dst[0:64, 1:8, :Cj], zb[64:128, 0:7, :])
                    act.copy(dst[0:64, 0:1, :Cj], zb[64:128, 7:8, :])

                if p0_mark[par]:
                    act.wait_ge(VEC.h, p0_mark[par])
                act_cos(tq3_2[par], q0)
                act_cos(tk3_2[par], k0)
                cos_marks[par] = ACTC.n
                act_roll(trlq_2[par], q0)
                act_roll(trlk_2[par], k0)
                act_cos(tv3_2[par], v0)
                act_roll(trlv_2[par], v0)
                prep_marks[par] = ACTC.n

            def quat_dve(i, half):
                Ci = CAs[i]
                c0 = C0s[i]
                hl = 1 if i > 0 else 0
                W = Ci + hl
                par = i % 2
                qkv = qkv2[par]
                p0b = p0b2[par]
                q0 = qkv[:, 0:8, hl:W]
                k0 = qkv[:, 8:16, hl:W]
                v0 = qkv[:, 16:24, hl:W]
                tq3, tk3, tv3 = tq3_2[par], tk3_2[par], tv3_2[par]
                trlq, trlk, trlv = trlq_2[par], trlk_2[par], trlv_2[par]
                T = lambda t: t[:, :, :Ci]

                def diff(dst, base):
                    if hl:
                        vec.tensor_tensor(T(dst), qkv[:, base:base + 8, 1:W],
                                          qkv[:, base:base + 8, 0:W - 1], ALU.subtract)
                    else:
                        vec.tensor_tensor(dst[:, :, 1:Ci], qkv[:, base:base + 8, 1:Ci],
                                          qkv[:, base:base + 8, 0:Ci - 1], ALU.subtract)
                        vec.memset(dst[:, :, 0:1], 0.0)

                def roll_into(dst, zb):
                    vec.tensor_copy(dst[64:128, :, :Ci], zb[0:64, :, :])
                    vec.tensor_copy(dst[0:64, 1:8, :Ci], zb[64:128, 0:7, :])
                    vec.tensor_copy(dst[0:64, 0:1, :Ci], zb[64:128, 7:8, :])

                if half == 0:
                    vec.wait_ge(ACTC.h, evac_marks[par])
                    diff(tq1, 0)
                    diff(tk1, 8)
                    vec.tensor_tensor(T(tm), q0, k0, ALU.mult)
                    vec.tensor_tensor(T(tt), T(tq1), T(tk1), ALU.mult)
                    vec.tensor_tensor(T(qk0), T(tm), T(tt), ALU.subtract)
                    vec.wait_ge(ACTC.h, prep_marks[par])
                    vec.tensor_tensor(T(qk1), T(trlq), T(trlk), ALU.mult)
                    vec.tensor_tensor(T(qk0), T(qk0), T(qk1), ALU.subtract)
                    vec.tensor_tensor(T(tt), T(tq3), T(tk3), ALU.mult)
                    vec.tensor_tensor(T(qk0), T(qk0), T(tt), ALU.subtract)
                    vec.tensor_tensor(T(qk1), q0, T(tk1), ALU.mult)
                    vec.tensor_tensor(T(tt), T(tq1), k0, ALU.mult)
                    vec.tensor_tensor(T(qk1), T(qk1), T(tt), ALU.add)
                    vec.tensor_tensor(T(tt), T(trlq), T(tk3), ALU.mult)
                    vec.tensor_tensor(T(qk1), T(qk1), T(tt), ALU.add)
                    vec.tensor_tensor(T(tt), T(tq3), T(trlk), ALU.mult)
                    vec.tensor_tensor(T(qk1), T(qk1), T(tt), ALU.subtract)
                else:
                    vec.tensor_tensor(T(qk2), q0, T(trlk), ALU.mult)
                    vec.tensor_tensor(T(tt), T(tq1), T(tk3), ALU.mult)
                    vec.tensor_tensor(T(qk2), T(qk2), T(tt), ALU.subtract)
                    vec.tensor_tensor(T(tt), T(trlq), k0, ALU.mult)
                    vec.tensor_tensor(T(qk2), T(qk2), T(tt), ALU.add)
                    vec.tensor_tensor(T(tt), T(tq3), T(tk1), ALU.mult)
                    vec.tensor_tensor(T(qk2), T(qk2), T(tt), ALU.add)
                    vec.tensor_tensor(T(qk3), q0, T(tk3), ALU.mult)
                    vec.tensor_tensor(T(tt), T(tq1), T(trlk), ALU.mult)
                    vec.tensor_tensor(T(qk3), T(qk3), T(tt), ALU.add)
                    vec.tensor_tensor(T(tt), T(trlq), T(tk1), ALU.mult)
                    vec.tensor_tensor(T(qk3), T(qk3), T(tt), ALU.subtract)
                    vec.tensor_tensor(T(tt), T(tq3), k0, ALU.mult)
                    vec.tensor_tensor(T(qk3), T(qk3), T(tt), ALU.add)
                    diff(tq1, 16)
                    vec.tensor_tensor(T(tm), T(qk0), v0, ALU.mult)
                    vec.tensor_tensor(T(tt), T(qk1), T(tq1), ALU.mult)
                    vec.tensor_tensor(T(tm), T(tm), T(tt), ALU.subtract)
                    vec.tensor_tensor(T(tt), T(qk2), T(trlv), ALU.mult)
                    vec.tensor_tensor(T(tm), T(tm), T(tt), ALU.subtract)
                    vec.tensor_tensor(T(tt), T(qk3), T(tv3), ALU.mult)
                    vec.wait_ge(LD.h, LD.n)
                    vec.tensor_tensor(p0b[:, :, :Ci], T(tm), T(tt), ALU.subtract)
                    p0_mark[par] = VEC.n
                    sync.wait_ge(VEC.h, VEC.n)
                    ld(p0_s[:, :, c0:c0 + Ci].transpose([1, 0, 2]), p0b[:, :, :Ci])

            front_top(0)
            front_dve(0)
            front_mm(0)
            for i in range(NCH):
                if i + 1 < NCH:
                    front_top(i + 1)
                quat_dve(i, 0)
                if i + 1 < NCH:
                    front_dve(i + 1)
                quat_dve(i, 1)
                if i + 1 < NCH:
                    front_mm(i + 1)
        bar()
        if l == 0:
            tap3("p0_0", p0_s)
            if ln1_s is not None:
                tap3("ln1_0", ln1_s)
            if qkv_s is not None:
                tap3("qkv_0", qkv_s)

        with contextlib.ExitStack() as stg, scope(f"B{l}"):
            Qsb = stg.enter_context(nc.sbuf_tensor(uname("Qsb"), [128, F, E], BF16))
            pw_sb = stg.enter_context(nc.sbuf_tensor(uname("pw_sb"), [128, F, E], BF16))
            wp_sb = stg.enter_context(nc.sbuf_tensor(uname("wp_sb"), [128, F, E], BF16))
            pbv = stg.enter_context(nc.sbuf_tensor(uname("pbv"), [128, 8], F32))
            modb = stg.enter_context(nc.sbuf_tensor(uname("modb"), [128, S], F32))
            bp0 = stg.enter_context(nc.sbuf_tensor(uname("bp0"), [128, F, CB], BF16))
            bct = stg.enter_context(nc.sbuf_tensor(uname("bct"), [128, F, CB], F32))
            bxa = stg.enter_context(nc.sbuf_tensor(uname("bxa"), [128, F, CB], F32))
            btp = stg.enter_context(nc.sbuf_tensor(uname("btp"), [128, F, CB], BF16))
            b_ps0 = stg.enter_context(nc.psum_tensor(uname("b_ps0"), [128, CB], F32))
            b_ps1 = stg.enter_context(nc.psum_tensor(uname("b_ps1"), [128, CB], F32))
            stage_start()
            ld(Qsb[:], cQ_h.rearrange("(k p) m -> p k m", p=128))
            ld(modb[:], cMOD_h[:])
            wld(pw_sb[:], pw_h[l].rearrange("(k p) m -> p k m", p=128))
            ld(pbv[:], pb_h[l][:])
            psB2 = [Bank(b_ps0), Bank(b_ps1)]
            ten.wait_ge(WL.h, WL.n)
            ten.wait_ge(LD.h, LD.n)
            for m in range(F):
                for half in range(2):
                    bk = psB2[half]
                    bk.wait()
                    n0 = half * 512
                    i = None
                    for k in range(F):
                        i = ten.matmul(bk.ps[:, :512], Qsb[:, k, m * 128:(m + 1) * 128],
                                       pw_sb[:, k, n0:n0 + 512],
                                       start=(k == 0), stop=(k == F - 1))
                    TEN.inc(i)
                    act.wait_ge(TEN.h, TEN.n)
                    bk.a = ACTC.inc(act.activation(
                        wp_sb[:, m, n0:n0 + 512], bk.ps[:, :512], AF.Copy))
            for ci in range(S // CB):
                c0 = ci * CB
                sync.wait_ge(VEC.h, VEC.n)
                sync.wait_ge(LD.h, LD.n)
                ld(bp0[:], p0_s[:, :, c0:c0 + CB].transpose([1, 0, 2]))
                ld(bct[:], cCT_h.rearrange("(f p) s -> p f s", p=128)[:, :, c0:c0 + CB])
                ld(bxa[:], xa_s[:, :, c0:c0 + CB].transpose([1, 0, 2]))
                for e in (ten, vec):
                    e.wait_ge(LD.h, LD.n)
                for m in range(F):
                    bk = psB2[m % 2]
                    bk.wait()
                    i = None
                    for k in range(F):
                        i = ten.matmul(bk.ps[:, :CB], Qsb[:, k, m * 128:(m + 1) * 128],
                                       bp0[:, k, :], start=(k == 0), stop=(k == F - 1))
                    TEN.inc(i)
                    vec.wait_ge(TEN.h, TEN.n)
                    bk.v = VEC.inc(vec.tensor_tensor(
                        btp[:, m, :], bk.ps[:, :CB], bct[:, m, :], ALU.mult))
                ten.wait_ge(VEC.h, VEC.n)
                for m in range(F):
                    bk = psB2[m % 2]
                    bk.wait()
                    i = None
                    for k in range(F):
                        i = ten.matmul(bk.ps[:, :CB], wp_sb[:, k, m * 128:(m + 1) * 128],
                                       btp[:, k, :], start=(k == 0), stop=(k == F - 1))
                    TEN.inc(i)
                    vec.wait_ge(TEN.h, TEN.n)
                    vec.tensor_tensor(bct[:, m, :], bk.ps[:, :CB],
                                      modb[:, c0:c0 + CB], ALU.mult)
                    bk.v = VEC.inc(vec.scalar_tensor_tensor(
                        bxa[:, m, :], bct[:, m, :], pbv[:, m:m + 1], bxa[:, m, :],
                        ALU.add, ALU.add))
                sync.wait_ge(VEC.h, VEC.n)
                ld(xa_s[:, :, c0:c0 + CB].transpose([1, 0, 2]), bxa[:])
        bar()
        if l == 0:
            tap3("xa_a0", xa_s)

        with contextlib.ExitStack() as stg, scope(f"D{l}"):
            fw_sb = stg.enter_context(nc.sbuf_tensor(uname("fw_sb"), [128, F, 4 * E], BF16))
            mwb = [stg.enter_context(nc.sbuf_tensor(uname("mwb"), [128, 32, 128], BF16))
                   for _ in range(3)]
            fbv = stg.enter_context(nc.sbuf_tensor(uname("fbv"), [128, 32], F32))
            mbv = stg.enter_context(nc.sbuf_tensor(uname("mbv"), [128, 8], F32))
            l2wv = stg.enter_context(nc.sbuf_tensor(uname("l2wv"), [128, 8], F32))
            l2bv = stg.enter_context(nc.sbuf_tensor(uname("l2bv"), [128, 8], F32))
            dx2 = [stg.enter_context(nc.sbuf_tensor(uname("dxA"), [128, F, CB], F32)),
                   stg.enter_context(nc.sbuf_tensor(uname("dxB"), [128, F, CB], F32))]
            dln2 = [stg.enter_context(nc.sbuf_tensor(uname("dlnA"), [128, F, CB], BF16)),
                    stg.enter_context(nc.sbuf_tensor(uname("dlnB"), [128, F, CB], BF16))]
            dxb = stg.enter_context(nc.sbuf_tensor(uname("dxb"), [128, F, CB], BF16))
            dgs = stg.enter_context(nc.sbuf_tensor(uname("dgs"), [128, F, CB], F32))
            dg = stg.enter_context(nc.sbuf_tensor(uname("dg"), [128, 32, CB], BF16))
            d_psA = stg.enter_context(nc.psum_tensor(uname("d_psA"), [1, CB], F32))
            d_psB = stg.enter_context(nc.psum_tensor(uname("d_psB"), [1, CB], F32))
            d_psR = stg.enter_context(nc.psum_tensor(uname("d_psR"), [128, CB], F32))
            d_psM = stg.enter_context(nc.psum_tensor(uname("d_psM"), [128, CB], F32))
            d_psF0 = stg.enter_context(nc.psum_tensor(uname("d_psF0"), [128, CB], F32))
            d_psF1 = stg.enter_context(nc.psum_tensor(uname("d_psF1"), [128, CB], F32))
            stage_start()
            wld(fw_sb[:], fw_h[l].rearrange("(k p) m -> p k m", p=128))
            mwT = mw_h[l].rearrange("(k p) m -> p k m", p=128)
            mw_cons = [0, 0, 0]
            mw_tok = [0, 0, 0]
            mw_next = [0]

            def mw_prefetch(upto):
                total = NCH_D * F if True else 0
                while mw_next[0] < min(upto, S // CB * F):
                    g = mw_next[0]
                    slot = g % 3
                    pool.wait_ge(TEN.h, mw_cons[slot])
                    wld(mwb[slot][:], mwT[:, :, (g % F) * 128:(g % F) * 128 + 128])
                    mw_tok[slot] = WL.n
                    mw_next[0] = g + 1
            ld(fbv[:], fb_h[l][:])
            ld(mbv[:], mb_h[l][:])
            ld(l2wv[:], l2w_h[l][:])
            ld(l2bv[:], l2b_h[l][:])
            psD = [Bank(d_psF0), Bank(d_psF1)]
            NCH_D = S // CB
            dln_mark = [0]
            mpe_mark = [0]

            def d_top(j):
                par = j % 2
                c0 = j * CB
                sync.wait_ge(VEC.h, VEC.n)
                sync.wait_ge(ACTC.h, ACTC.n)
                sync.wait_ge(LD.h, LD.n)
                ld(dx2[par][:], xa_s[:, :, c0:c0 + CB].transpose([1, 0, 2]))
                for e in (act, ten):
                    e.wait_ge(LD.h, LD.n)
                    e.wait_ge(WL.h, WL.n)
                ln_stats_top(dx2[par][:], CB, dxb[:], dln2[par][:], d_psA, d_psB)

            def d_ln(j):
                par = j % 2
                psRb, psMb = ln_stats_fin(CB, d_psA, d_psB, d_psR, d_psM)
                gam = l2wv[:].unsqueeze(2).broadcast_to([128, F, CB])
                bet = l2bv[:].unsqueeze(2).broadcast_to([128, F, CB])
                vec.tensor_tensor(dgs[:], dx2[par][:], psRb, ALU.mult)
                vec.tensor_tensor(dgs[:], dgs[:], psMb, ALU.subtract)
                vec.tensor_tensor(dgs[:], dgs[:], gam, ALU.mult)
                vec.tensor_tensor(dln2[par][:], dgs[:], bet, ALU.add)
                dln_mark[0] = VEC.n

            def d_fc(i):
                par = i % 2
                mw_prefetch(i * F + 2)
                ten.wait_ge(VEC.h, dln_mark[0])
                for mo in range(32):
                    bk = psD[mo % 2]
                    bk.wait()
                    ii = None
                    for k in range(F):
                        ii = ten.matmul(bk.ps[:, :CB], fw_sb[:, k, mo * 128:(mo + 1) * 128],
                                        dln2[par][:, k, :], start=(k == 0), stop=(k == F - 1))
                    TEN.inc(ii)
                    act.wait_ge(TEN.h, TEN.n)
                    if mo == 0 and mpe_mark[0]:
                        act.wait_ge(TEN.h, mpe_mark[0])
                    act.activation(dg[:, mo, :], bk.ps[:, :CB], AF.Gelu,
                                   bias=fbv[:, mo:mo + 1])
                    bk.a = ACTC.n

            def d_mproj(i):
                par = i % 2
                c0 = i * CB
                ten.wait_ge(ACTC.h, ACTC.n)
                for m2 in range(F):
                    g = i * F + m2
                    mw_prefetch(g + 3)
                    slot = g % 3
                    bk = psD[m2 % 2]
                    bk.wait()
                    ten.wait_ge(WL.h, mw_tok[slot])
                    ii = None
                    for mo in range(32):
                        ii = ten.matmul(bk.ps[:, :CB], mwb[slot][:, mo, 0:128],
                                        dg[:, mo, :], start=(mo == 0), stop=(mo == 31))
                    mw_cons[slot] = TEN.inc(ii)
                    vec.wait_ge(TEN.h, TEN.n)
                    vec.scalar_tensor_tensor(
                        dx2[par][:, m2, :], bk.ps[:, :CB], mbv[:, m2:m2 + 1],
                        dx2[par][:, m2, :], ALU.add, ALU.add)
                    bk.v = VEC.n
                mpe_mark[0] = mw_cons[(i * F + F - 1) % 3]
                sync.wait_ge(VEC.h, VEC.n)
                ld(xa_s[:, :, c0:c0 + CB].transpose([1, 0, 2]), dx2[par][:])

            d_top(0)
            d_ln(0)
            for i in range(NCH_D):
                if i + 1 < NCH_D:
                    d_top(i + 1)
                d_fc(i)
                if i + 1 < NCH_D:
                    d_ln(i + 1)
                d_mproj(i)
        bar()
        if l == 0:
            tap3("xa_0", xa_s)
        if l == NL - 1:
            tap3("xa_3", xa_s)

    with contextlib.ExitStack() as stg, scope("R"):
        gbv = stg.enter_context(nc.sbuf_tensor(uname("gbv"), [NE, 1], F32))
        sfc = stg.enter_context(nc.sbuf_tensor(uname("sfc"), [NE, S], F32))
        sfcT = stg.enter_context(nc.sbuf_tensor(uname("sfcT"), [128, 16, NE], F32))
        r_gs = stg.enter_context(nc.sbuf_tensor(uname("r_gs"), [128, 16, 4], F32))
        r_t1 = stg.enter_context(nc.sbuf_tensor(uname("r_t1"), [128, 16, 4], F32))
        r_t2 = stg.enter_context(nc.sbuf_tensor(uname("r_t2"), [128, 16, 4], F32))
        r_r1 = stg.enter_context(nc.sbuf_tensor(uname("r_r1"), [128, 16], F32))
        r_r2 = stg.enter_context(nc.sbuf_tensor(uname("r_r2"), [128, 16], F32))
        r_ms = stg.enter_context(nc.sbuf_tensor(uname("r_ms"), [128, 16, NE], F32))
        r_tc = stg.enter_context(nc.sbuf_tensor(uname("r_tc"), [128, 16, NE], F32))
        r_cm = stg.enter_context(nc.sbuf_tensor(uname("r_cm"), [128, 16, NE], F32))
        r_ps = stg.enter_context(nc.psum_tensor(uname("r_ps"), [128, 16], F32))
        r_psT = stg.enter_context(nc.psum_tensor(uname("r_psT"), [16, 128], F32))
        stage_start()
        ld(gbv[:], gb_h[:])
        act.wait_ge(LD.h, LD.n)
        ACTC.inc(act.activation(sfc[:], logits[:], AF.Sigmoid))
        vec.wait_ge(ACTC.h, ACTC.n)
        VEC.inc(vec.tensor_scalar(sfc[:], sfc[:], gbv[:], None, ALU.add))
        ten.wait_ge(VEC.h, VEC.n)
        for t in range(16):
            if t > 0:
                ten.wait_ge(VEC.h, VEC.n)
            TEN.inc(ten.transpose(r_ps[:], sfc[:, t * 128:(t + 1) * 128], ident[0:16, 0:16]))
            vec.wait_ge(TEN.h, TEN.n)
            VEC.inc(vec.tensor_copy(sfcT[:, t, :], r_ps[:]))
        if "sfcT" in taps:
            sync.wait_ge(VEC.h, VEC.n)
            ld(taps["sfcT"][:], sfcT[:].rearrange("p t e -> p (t e)"))
        s4 = sfcT[:].rearrange("p t (g j) -> p t g j", g=4)
        vec.reduce_max(r_gs[:], s4, axis=AX.X)
        m1b = r_gs[:].unsqueeze(3).broadcast_to([128, 16, 4, 4])
        t4 = r_ms[:].rearrange("p t (g j) -> p t g j", g=4)
        vec.tensor_tensor(t4, s4, m1b, ALU.is_ge)
        vec.scalar_tensor_tensor(t4, t4, NEG, s4, ALU.mult, ALU.add)
        vec.reduce_max(r_t1[:], t4, axis=AX.X)
        vec.tensor_tensor(r_gs[:], r_gs[:], r_t1[:], ALU.add)
        vec.reduce_max(r_r1[:], r_gs[:], axis=AX.X)
        g1b = r_r1[:].unsqueeze(2).broadcast_to([128, 16, 4])
        vec.tensor_tensor(r_t1[:], r_gs[:], g1b, ALU.is_ge)
        vec.scalar_tensor_tensor(r_t1[:], r_t1[:], NEG, r_gs[:], ALU.mult, ALU.add)
        vec.reduce_max(r_r2[:], r_t1[:], axis=AX.X)
        r2b = r_r2[:].unsqueeze(2).broadcast_to([128, 16, 4])
        vec.tensor_tensor(r_t1[:], r_gs[:], r2b, ALU.is_ge)
        mb4 = r_t1[:].unsqueeze(3).broadcast_to([128, 16, 4, 4])
        vec.tensor_tensor(r_ms[:].rearrange("p t (g j) -> p t g j", g=4),
                          s4, mb4, ALU.mult)
        if "ms" in taps:
            sync.wait_ge(VEC.h, VEC.n)
            ld(taps["ms"][:], r_ms[:].rearrange("p t e -> p (t e)"))
        vec.tensor_copy(r_tc[:], r_ms[:])
        for it in range(4):
            vec.reduce_max(r_r1[:], r_tc[:], axis=AX.X)
            if it < 3:
                r1b = r_r1[:].unsqueeze(2).broadcast_to([128, 16, NE])
                vec.tensor_tensor(r_cm[:], r_tc[:], r1b, ALU.is_ge)
                vec.scalar_tensor_tensor(r_tc[:], r_cm[:], NEG, r_tc[:],
                                         ALU.mult, ALU.add)
        m4b = r_r1[:].unsqueeze(2).broadcast_to([128, 16, NE])
        vec.tensor_tensor(r_cm[:], r_ms[:], m4b, ALU.is_ge)
        vec.tensor_tensor(r_cm[:], r_cm[:], r_ms[:], ALU.mult)
        vec.reduce_sum(r_r1[:], r_cm[:], axis=AX.X)
        vec.tensor_scalar(r_r1[:], r_r1[:], 1e-20, None, ALU.add)
        vec.reciprocal(r_r2[:], r_r1[:])
        vec.tensor_scalar(r_r2[:], r_r2[:], 2.5, None, ALU.mult)
        wb = r_r2[:].unsqueeze(2).broadcast_to([128, 16, NE])
        vec.tensor_tensor(r_cm[:], r_cm[:], wb, ALU.mult)
        if "rcm" in taps:
            sync.wait_ge(VEC.h, VEC.n)
            ld(taps["rcm"][:], r_cm[:].rearrange("p t e -> p (t e)"))
            sync.wait_ge(LD.h, LD.n)
        ten.wait_ge(VEC.h, VEC.n)
        for t in range(16):
            if t > 0:
                ten.wait_ge(VEC.h, VEC.n)
            TEN.inc(ten.transpose(r_psT[:], r_cm[:, t, :], ident[:]))
            vec.wait_ge(TEN.h, TEN.n)
            VEC.inc(vec.tensor_copy(sfc[:, t * 128:(t + 1) * 128], r_psT[:]))
        sync.wait_ge(VEC.h, VEC.n)
        ld(comb_s[:], sfc[:, :])
    bar()
    if "comb" in taps:
        pool.wait_ge(LD.h, LD.n)
        wld(taps["comb"][:], comb_s[:])
        pool.wait_ge(WL.h, WL.n)
        bar()

    with contextlib.ExitStack() as stg, scope("SH"):
        swg_sb = stg.enter_context(nc.sbuf_tensor(uname("swg_sb"), [128, F, E], BF16))
        swu_sb = stg.enter_context(nc.sbuf_tensor(uname("swu_sb"), [128, F, E], BF16))
        swd_sb = stg.enter_context(nc.sbuf_tensor(uname("swd_sb"), [128, F, E], BF16))
        shx = stg.enter_context(nc.sbuf_tensor(uname("shx"), [128, F, CB], BF16))
        sha = stg.enter_context(nc.sbuf_tensor(uname("sha"), [128, F, CB], F32))
        shg = stg.enter_context(nc.sbuf_tensor(uname("shg"), [128, F, CB], BF16))
        shi = stg.enter_context(nc.sbuf_tensor(uname("shi"), [128, F, CB], BF16))
        sh_ps0 = stg.enter_context(nc.psum_tensor(uname("sh_ps0"), [128, CB], F32))
        sh_ps1 = stg.enter_context(nc.psum_tensor(uname("sh_ps1"), [128, CB], F32))
        stage_start()
        wld(swg_sb[:], swg_h.rearrange("(k p) m -> p k m", p=128))
        wld(swu_sb[:], swu_h.rearrange("(k p) m -> p k m", p=128))
        wld(swd_sb[:], swd_h.rearrange("(k p) m -> p k m", p=128))
        psS = [Bank(sh_ps0), Bank(sh_ps1)]
        for ci in range(S // CB):
            c0 = ci * CB
            sync.wait_ge(VEC.h, VEC.n)
            sync.wait_ge(ACTC.h, ACTC.n)
            sync.wait_ge(LD.h, LD.n)
            ld(shx[:], xnb_s[:, :, c0:c0 + CB].transpose([1, 0, 2]))
            ld(sha[:], xa_s[:, :, c0:c0 + CB].transpose([1, 0, 2]))
            for e in (act, ten, vec):
                e.wait_ge(LD.h, LD.n)
                e.wait_ge(WL.h, WL.n)
            for m in range(F):
                bk = psS[m % 2]
                bk.wait()
                i = None
                for k in range(F):
                    i = ten.matmul(bk.ps[:, :CB], swg_sb[:, k, m * 128:(m + 1) * 128],
                                   shx[:, k, :], start=(k == 0), stop=(k == F - 1))
                TEN.inc(i)
                act.wait_ge(TEN.h, TEN.n)
                bk.a = ACTC.inc(act.activation(shg[:, m, :], bk.ps[:, :CB], AF.Silu))
            ten.wait_ge(ACTC.h, ACTC.n)
            for m in range(F):
                bk = psS[m % 2]
                bk.wait()
                i = None
                for k in range(F):
                    i = ten.matmul(bk.ps[:, :CB], swu_sb[:, k, m * 128:(m + 1) * 128],
                                   shx[:, k, :], start=(k == 0), stop=(k == F - 1))
                TEN.inc(i)
                vec.wait_ge(TEN.h, TEN.n)
                bk.v = VEC.inc(vec.tensor_tensor(
                    shi[:, m, :], bk.ps[:, :CB], shg[:, m, :], ALU.mult))
            ten.wait_ge(VEC.h, VEC.n)
            for m in range(F):
                bk = psS[m % 2]
                bk.wait()
                i = None
                for k in range(F):
                    i = ten.matmul(bk.ps[:, :CB], swd_sb[:, k, m * 128:(m + 1) * 128],
                                   shi[:, k, :], start=(k == 0), stop=(k == F - 1))
                TEN.inc(i)
                vec.wait_ge(TEN.h, TEN.n)
                bk.v = VEC.inc(vec.tensor_tensor(
                    sha[:, m, :], bk.ps[:, :CB], sha[:, m, :], ALU.add))
            sync.wait_ge(VEC.h, VEC.n)
            ld(xa_s[:, :, c0:c0 + CB].transpose([1, 0, 2]), sha[:])
    bar()

    with contextlib.ExitStack() as stg, scope("EX"):
        ewr = [stg.enter_context(nc.sbuf_tensor(uname("ewr"), [128, 2, F, 512], BF16))
               for _ in range(2)]
        eur = [stg.enter_context(nc.sbuf_tensor(uname("eur"), [128, 2, F, 512], BF16))
               for _ in range(2)]
        edr = [stg.enter_context(nc.sbuf_tensor(uname("edr"), [128, 2, 4, E], BF16))
               for _ in range(2)]
        ece = stg.enter_context(nc.sbuf_tensor(uname("ece"), [128, 2, CB], F32))
        exn = stg.enter_context(nc.sbuf_tensor(uname("exn"), [128, F, CB], BF16))
        exa = stg.enter_context(nc.sbuf_tensor(uname("exa"), [128, F, CB], F32))
        esg = stg.enter_context(nc.sbuf_tensor(uname("esg"), [128, 4, CB], BF16))
        eic = stg.enter_context(nc.sbuf_tensor(uname("eic"), [128, 2, 4, CB], BF16))
        e_ps0 = stg.enter_context(nc.psum_tensor(uname("e_ps0"), [128, CB], F32))
        e_ps1 = stg.enter_context(nc.psum_tensor(uname("e_ps1"), [128, CB], F32))
        stage_start()
        psE = [Bank(e_ps0), Bank(e_ps1)]
        ring_cons = [0, 0]
        ring_tok = [0, 0]

        def ex_load(g):
            slot = g % 2
            s = g % 8
            pool.wait_ge(TEN.h, ring_cons[slot])
            for j in range(2):
                e_id = s * 2 + j
                wld(ewr[slot][:, j, :, :], wg_h[e_id].rearrange("(k p) m -> p k m", p=128))
                wld(eur[slot][:, j, :, :], wu_h[e_id].rearrange("(k p) m -> p k m", p=128))
                wld(edr[slot][:, j, :, :], wd_h[e_id].rearrange("(k p) m -> p k m", p=128))
            ring_tok[slot] = WL.n

        ex_load(0)
        NCHX = S // CB
        for ci in range(NCHX):
            c0 = ci * CB
            sync.wait_ge(VEC.h, VEC.n)
            sync.wait_ge(ACTC.h, ACTC.n)
            sync.wait_ge(LD.h, LD.n)
            ld(exn[:], xnb_s[:, :, c0:c0 + CB].transpose([1, 0, 2]))
            ld(exa[:], xa_s[:, :, c0:c0 + CB].transpose([1, 0, 2]))
            xtok = LD.n
            for e in (act, ten, vec):
                e.wait_ge(LD.h, xtok)
            for s in range(8):
                g = ci * 8 + s
                slot = g % 2
                if g + 1 < NCHX * 8:
                    ex_load(g + 1)
                for j in range(2):
                    e_id = s * 2 + j
                    sync.wait_ge(VEC.h, VEC.n)
                    ld(ece[:, j, :], comb_s[e_id:e_id + 1, c0:c0 + CB]
                       .broadcast_to([128, CB]))
                etok = LD.n
                vec.wait_ge(LD.h, etok)
                ten.wait_ge(WL.h, ring_tok[slot])
                last_mm = 0
                for j in range(2):
                    for m in range(4):
                        bk = psE[m % 2]
                        bk.wait()
                        i = None
                        for k in range(F):
                            i = ten.matmul(bk.ps[:, :CB],
                                           ewr[slot][:, j, k, m * 128:(m + 1) * 128],
                                           exn[:, k, :], start=(k == 0),
                                           stop=(k == F - 1))
                        TEN.inc(i)
                        act.wait_ge(TEN.h, TEN.n)
                        act.activation(esg[:, m, :], bk.ps[:, :CB], AF.Silu)
                        bk.a = ACTC.n
                    for m in range(4):
                        bk = psE[m % 2]
                        bk.wait()
                        i = None
                        for k in range(F):
                            i = ten.matmul(bk.ps[:, :CB],
                                           eur[slot][:, j, k, m * 128:(m + 1) * 128],
                                           exn[:, k, :], start=(k == 0),
                                           stop=(k == F - 1))
                        TEN.inc(i)
                        vec.wait_ge(TEN.h, TEN.n)
                        vec.tensor_tensor(eic[:, j, m, :], bk.ps[:, :CB],
                                          esg[:, m, :], ALU.mult)
                        vec.tensor_tensor(eic[:, j, m, :], eic[:, j, m, :],
                                          ece[:, j, :], ALU.mult)
                        bk.v = VEC.n
                    act.wait_ge(VEC.h, VEC.n)
                ten.wait_ge(VEC.h, VEC.n)
                for m2 in range(F):
                    bk = psE[m2 % 2]
                    bk.wait()
                    i = None
                    for j in range(2):
                        for k in range(4):
                            i = ten.matmul(bk.ps[:, :CB],
                                           edr[slot][:, j, k, m2 * 128:(m2 + 1) * 128],
                                           eic[:, j, k, :],
                                           start=(j == 0 and k == 0),
                                           stop=(j == 1 and k == 3))
                    last_mm = TEN.inc(i)
                    vec.wait_ge(TEN.h, TEN.n)
                    vec.tensor_tensor(exa[:, m2, :], bk.ps[:, :CB],
                                      exa[:, m2, :], ALU.add)
                    bk.v = VEC.n
                ring_cons[slot] = last_mm
            sync.wait_ge(VEC.h, VEC.n)
            ld(xa_s[:, :, c0:c0 + CB].transpose([1, 0, 2]), exa[:])
    bar()
    tap3("moe", xa_s)

    with contextlib.ExitStack() as stg, scope("FIN"):
        fxa = stg.enter_context(nc.sbuf_tensor(uname("fxa"), [128, F, 128], F32))
        fx = stg.enter_context(nc.sbuf_tensor(uname("fx"), [128, F, 128], F32))
        frow = stg.enter_context(nc.sbuf_tensor(uname("frow"), [128, E], F32))
        f_ps = stg.enter_context(nc.psum_tensor(uname("f_ps"), [128, 128], F32))
        stage_start()
        for t in range(S // 128):
            sync.wait_ge(VEC.h, VEC.n)
            ld(fxa[:], xa_s[:, :, t * 128:(t + 1) * 128].transpose([1, 0, 2]))
            ld(fx[:], xT_s[:, :, t * 128:(t + 1) * 128].transpose([1, 0, 2]))
            vec.wait_ge(LD.h, LD.n)
            VEC.inc(vec.tensor_tensor(fxa[:], fxa[:], fx[:], ALU.add))
            ten.wait_ge(VEC.h, VEC.n)
            for f in range(F):
                if f > 0:
                    ten.wait_ge(VEC.h, VEC.n)
                TEN.inc(ten.transpose(f_ps[:], fxa[:, f, :], ident[:]))
                vec.wait_ge(TEN.h, TEN.n)
                VEC.inc(vec.tensor_copy(frow[:, f * 128:(f + 1) * 128], f_ps[:]))
            sync.wait_ge(VEC.h, VEC.n)
            ld(out_h[t * 128:(t + 1) * 128, :], frow[:])
        sync.wait_ge(LD.h, LD.n)
    bar()

    stack.close()
    return nc


_PROGRAM_CACHE = {}


def kernel(**inputs):
    consts = _build_constants()
    perm = _perm_qkv()

    x = np.ascontiguousarray(np.asarray(inputs["L0_attn_x"], np.float32))
    aw = np.asarray(inputs["L0_attn_attn_w"], np.float32)
    ab = np.asarray(inputs["L0_attn_attn_b"], np.float32)
    base = {}
    for l in range(NL):
        base[f"aw{l}"] = np.ascontiguousarray(aw[l][:, perm])
        base[f"ab{l}"] = _cols(ab[l][perm], 24)
        base[f"pw{l}"] = np.ascontiguousarray(np.asarray(inputs["L0_attn_aproj_w"][l], np.float32))
        base[f"pb{l}"] = _cols(inputs["L0_attn_aproj_b"][l], 8)
        base[f"fw{l}"] = np.ascontiguousarray(np.asarray(inputs["L0_attn_fc_w"][l], np.float32))
        base[f"fb{l}"] = _cols(inputs["L0_attn_fc_b"][l], 32)
        base[f"mw{l}"] = np.ascontiguousarray(np.asarray(inputs["L0_attn_mproj_w"][l], np.float32))
        base[f"mb{l}"] = _cols(inputs["L0_attn_mproj_b"][l], 8)
        base[f"l1w{l}"] = _cols(inputs["L0_attn_ln1_w"][l], 8)
        base[f"l1b{l}"] = _cols(inputs["L0_attn_ln1_b"][l], 8)
        base[f"l2w{l}"] = _cols(inputs["L0_attn_ln2_w"][l], 8)
        base[f"l2b{l}"] = _cols(inputs["L0_attn_ln2_b"][l], 8)
    base["gk"] = np.ascontiguousarray(np.asarray(inputs["L0_ffn_gate_kernel"], np.float32))
    base["gb"] = np.ascontiguousarray(np.asarray(inputs["L0_ffn_gate_bias"], np.float32).reshape(NE, 1))
    base["wg"] = np.ascontiguousarray(np.asarray(inputs["L0_ffn_w_gate"], np.float32))
    base["wu"] = np.ascontiguousarray(np.asarray(inputs["L0_ffn_w_up"], np.float32))
    base["wd"] = np.ascontiguousarray(np.asarray(inputs["L0_ffn_w_down"], np.float32))
    base["swg"] = np.ascontiguousarray(np.asarray(inputs["L0_ffn_sw_gate"], np.float32))
    base["swu"] = np.ascontiguousarray(np.asarray(inputs["L0_ffn_sw_up"], np.float32))
    base["swd"] = np.ascontiguousarray(np.asarray(inputs["L0_ffn_sw_down"], np.float32))
    base.update(consts)

    nc = build_program()
    in_maps = []
    for c in range(8):
        m = dict(base)
        m["x"] = np.ascontiguousarray(x[c])
        in_maps.append(m)
    res = run_bass_kernel_spmd(nc, in_maps, core_ids=list(range(8)))
    out = np.stack([np.asarray(r["out"], np.float32) for r in res.results], axis=0)
    if DEBUG_TAPS:
        kernel._taps = [{k: np.asarray(v) for k, v in r.items() if k.startswith("tap_")}
                        for r in res.results]
    return out


# version 22
# speedup vs baseline: 1.0374x; 1.0001x over previous
import numpy as np
import concourse.bass as bass
import concourse.mybir as mybir
from concourse.bass_utils import run_bass_kernel_spmd

F32 = mybir.dt.float32
BF16 = mybir.dt.bfloat16
AF = mybir.ActivationFunctionType
ALU = mybir.AluOpType
AX = mybir.AxisListType

S = 2048
E = 1024
F = 8
H, HD = 16, 64
NL = 4
NE = 16
EPS = 1e-5
CA = 256
CB = 512
NEG = -1.0e30

DEBUG_TAPS = []
SIM_STRICT = False


def _build_constants():
    import ml_dtypes
    ks = np.fft.fftfreq(S)
    kh = np.fft.fftfreq(H)
    kd = np.fft.fftfreq(HD)
    km = np.sqrt(ks[:, None, None] ** 2 + kh[None, :, None] ** 2
                 + kd[None, None, :] ** 2 + 1e-10)
    c = np.cos(np.arctan(np.log(km + 1e-10)))
    hh = np.arange(H)[:, None]
    dd = np.arange(HD)[None, :]
    cols, eigs, seen = [], [], set()
    for fh in range(H):
        for fd in range(HD):
            if (fh, fd) in seen:
                continue
            ph, pd = (-fh) % H, (-fd) % HD
            seen.add((fh, fd)); seen.add((ph, pd))
            phase = 2 * np.pi * (fh * hh / H + fd * dd / HD)
            if (ph, pd) == (fh, fd):
                cols.append((np.cos(phase) / np.sqrt(E)).reshape(-1))
                eigs.append(c[:, fh, fd])
            else:
                cols.append((np.cos(phase) * np.sqrt(2.0 / E)).reshape(-1))
                eigs.append(c[:, fh, fd])
                cols.append((np.sin(phase) * np.sqrt(2.0 / E)).reshape(-1))
                eigs.append(c[:, fh, fd])
    Q = np.stack(cols, axis=1)
    ctil = np.stack(eigs, axis=0)
    lam = np.arange(S) / S
    t = np.arange(S) * 0.1
    mod = 1.0 + 0.1 * (np.sin(t + lam) * np.cos(t - 2.0 * lam + 0.5 * lam * lam))
    return {
        "cQ": np.ascontiguousarray(Q.astype(np.float32).astype(ml_dtypes.bfloat16)),
        "cCT": np.ascontiguousarray(ctil.astype(np.float32)),
        "cMOD": np.ascontiguousarray(np.broadcast_to(
            mod.astype(np.float32)[None, :], (128, S))),
        "cID": np.eye(128, dtype=np.float32),
        "cONEB": np.ones((128, 1), dtype=ml_dtypes.bfloat16),
        "cONER": np.ones((1, 128), dtype=np.float32),
    }


def _perm_qkv():
    idx = np.arange(3 * E).reshape(H, 3, HD)
    return np.concatenate([idx[:, 0, :].reshape(-1),
                           idx[:, 1, :].reshape(-1),
                           idx[:, 2, :].reshape(-1)])


def _cols(v, ntiles):
    return np.ascontiguousarray(np.asarray(v, np.float32).reshape(ntiles, 128).T)


class Ctr:
    def __init__(self, nc, name, step):
        self.h = nc.alloc_semaphore(name)
        self.n = 0
        self.step = step
        self.last = None

    def inc(self, instr):
        if instr is self.last:
            return self.n
        instr.then_inc(self.h, self.step)
        self.n += self.step
        self.last = instr
        return self.n


_AUTO_OPS = {
    "reciprocal",
    "tensor_tensor", "tensor_scalar", "tensor_single_scalar", "tensor_copy",
    "tensor_add", "tensor_sub", "tensor_mul", "tensor_max", "memset", "memzero",
    "reciprocal", "reduce_max", "reduce_sum", "tensor_reduce",
    "scalar_tensor_tensor", "select", "activation", "copy", "mul", "add",
    "square", "sqrt", "sign",
}


class MarkedEng:
    def __init__(self, eng, ctr):
        self._e = eng
        self._c = ctr

    def __getattr__(self, name):
        f = getattr(self._e, name)
        if name in _AUTO_OPS:
            c = self._c
            eng = self._e
            always = name == "reciprocal"

            def g(*a, **k):
                i = f(*a, **k)
                c.inc(i)
                small = SIM_STRICT
                if not small:
                    try:
                        small = a[0].free_size() < 256 or a[0].partition_size() == 1
                    except Exception:
                        small = True
                if always or small:
                    eng.wait_ge(c.h, c.n)
                return i
            return g
        return f


def build_program():
    nc = bass.Bass("TRN2", target_bir_lowering=False)
    sync, ten, vec, act, pool = nc.sync, nc.tensor, nc.vector, nc.scalar, nc.gpsimd

    def din(name, shape, dtype=F32):
        return nc.dram_tensor(name, list(shape), dtype, kind="ExternalInput").ap()

    x_h = din("x", (S, E))
    aw_h = [din(f"aw{l}", (E, 3 * E)) for l in range(NL)]
    ab_h = [din(f"ab{l}", (128, 24)) for l in range(NL)]
    pw_h = [din(f"pw{l}", (E, E)) for l in range(NL)]
    pb_h = [din(f"pb{l}", (128, 8)) for l in range(NL)]
    fw_h = [din(f"fw{l}", (E, 4 * E)) for l in range(NL)]
    fb_h = [din(f"fb{l}", (128, 32)) for l in range(NL)]
    mw_h = [din(f"mw{l}", (4 * E, E)) for l in range(NL)]
    mb_h = [din(f"mb{l}", (128, 8)) for l in range(NL)]
    l1w_h = [din(f"l1w{l}", (128, 8)) for l in range(NL)]
    l1b_h = [din(f"l1b{l}", (128, 8)) for l in range(NL)]
    l2w_h = [din(f"l2w{l}", (128, 8)) for l in range(NL)]
    l2b_h = [din(f"l2b{l}", (128, 8)) for l in range(NL)]
    gk_h = din("gk", (E, NE))
    gb_h = din("gb", (NE, 1))
    wg_h = din("wg", (NE, E, 512))
    wu_h = din("wu", (NE, E, 512))
    wd_h = din("wd", (NE, 512, E))
    swg_h = din("swg", (E, E))
    swu_h = din("swu", (E, E))
    swd_h = din("swd", (E, E))
    cQ_h = din("cQ", (E, E), BF16)
    cCT_h = din("cCT", (E, S))
    cMOD_h = din("cMOD", (128, S))
    cID_h = din("cID", (128, 128))
    cONEB_h = din("cONEB", (128, 1), BF16)
    cONER_h = din("cONER", (1, 128))

    out_h = nc.dram_tensor("out", [S, E], F32, kind="ExternalOutput").ap()

    def scr(name, shape, dtype=F32):
        return nc.dram_tensor(name, list(shape), dtype, kind="Internal").ap()

    xT_s = scr("xT_s", (F, 128, S))
    xa_s = scr("xa_s", (F, 128, S))
    xnb_s = scr("xnb_s", (F, 128, S), BF16)
    p0_s = scr("p0_s", (F, 128, S), BF16)
    comb_s = scr("comb_s", (NE, S))
    ln1_s = scr("ln1_s", (F, 128, S), BF16) if "ln1_0" in DEBUG_TAPS else None
    qkv_s = scr("qkv_s", (24, 128, S)) if "qkv_0" in DEBUG_TAPS else None

    taps = {}
    _tapshape = {"comb": [NE, S], "qkv_0": [24, 128, S], "sfcT": [128, 256],
                 "ms": [128, 256], "rcm": [128, 256]}
    for t in DEBUG_TAPS:
        shp = _tapshape.get(t, [F, 128, S])
        taps[t] = nc.dram_tensor("tap_" + t, shp, F32, kind="ExternalOutput").ap()

    LD = Ctr(nc, "cLD", 16)
    WL = Ctr(nc, "cWL", 16)
    TEN = Ctr(nc, "cTEN", 1)
    VEC = Ctr(nc, "cVEC", 1)
    ACTC = Ctr(nc, "cACT", 1)
    POOLC = Ctr(nc, "cPOOL", 1)
    vec = MarkedEng(vec, VEC)
    act = MarkedEng(act, ACTC)
    plc = MarkedEng(pool, POOLC)

    def ld(dst, src):
        if SIM_STRICT:
            sync.wait_ge(LD.h, LD.n)
        sync.dma_start(dst, src).then_inc(LD.h, 16)
        LD.n += 16

    def wld(dst, src):
        if SIM_STRICT:
            pool.wait_ge(WL.h, WL.n)
        pool.dma_start(dst, src).then_inc(WL.h, 16)
        WL.n += 16

    def bar():
        nc.all_engine_barrier()

    import contextlib as _ctx

    @_ctx.contextmanager
    def scope(nm):
        with nc.named_scope(nm, notify=True):
            yield

    def stage_start():
        for e in (sync, pool, ten, nc.vector, nc.scalar):
            e.wait_ge(LD.h, LD.n)
            e.wait_ge(WL.h, WL.n)

    class Bank:
        __slots__ = ("ps", "a", "v")
        def __init__(self, ps):
            self.ps = ps
            self.a = 0
            self.v = 0

        def wait(self):
            if self.a:
                ten.wait_ge(ACTC.h, self.a)
            if self.v:
                ten.wait_ge(VEC.h, self.v)

    def tap3(name, src_scr):
        if name in taps:
            bar()
            pool.wait_ge(LD.h, LD.n)
            wld(taps[name][:], src_scr[:])
            pool.wait_ge(WL.h, WL.n)
            bar()

    import contextlib
    stack = contextlib.ExitStack()
    _names = {}

    def uname(n):
        _names[n] = _names.get(n, -1) + 1
        return f"{n}_{_names[n]}" if _names[n] else n

    def sb(name, shape, dtype=F32):
        return stack.enter_context(nc.sbuf_tensor(name, list(shape), dtype))

    ident = sb("ident", (128, 128))
    oneb = sb("oneb", (128, 1), BF16)
    oner = sb("oner", (1, 128))
    logits = sb("logits", (NE, S))
    rows_mu = sb("rows_mu", (1, 520))
    rows_rs = sb("rows_rs", (1, 520))
    rows_tp = sb("rows_tp", (1, 520))

    epsv = sb("epsv", (128, 1))
    vec.memset(epsv[:], float(EPS))
    VEC.inc(vec.engine_nop()) if False else None
    ld(ident[:], cID_h[:])
    ld(oneb[:], cONEB_h[:])
    ld(oner[:], cONER_h[:])

    with contextlib.ExitStack() as stg, scope("IN"):
        in_rows = stg.enter_context(nc.sbuf_tensor(uname("in_rows"), [128, E], F32))
        in_cols = stg.enter_context(nc.sbuf_tensor(uname("in_cols"), [128, F, 128], F32))
        in_ps = stg.enter_context(nc.psum_tensor(uname("in_ps"), [128, 128], F32))
        for t in range(S // 128):
            sync.wait_ge(VEC.h, VEC.n)
            ld(in_rows[:], x_h[t * 128:(t + 1) * 128, :])
            ten.wait_ge(LD.h, LD.n)
            for f in range(F):
                if f > 0:
                    ten.wait_ge(VEC.h, VEC.n)
                TEN.inc(ten.transpose(in_ps[:], in_rows[:, f * 128:(f + 1) * 128],
                                      ident[:]))
                vec.wait_ge(TEN.h, TEN.n)
                VEC.inc(vec.tensor_copy(in_cols[:, f, :], in_ps[:]))
            sync.wait_ge(VEC.h, VEC.n)
            ld(xT_s[:, :, t * 128:(t + 1) * 128].transpose([1, 0, 2]), in_cols[:])
    bar()

    stats_copy_mark = [0]

    def ln_stats_top(src3, C, xb, xq, psA, psB):
        act.wait_ge(VEC.h, VEC.n)
        act.activation(xb, src3, AF.Copy)
        act.activation(xq, src3, AF.Square)
        stats_copy_mark[0] = ACTC.n
        ten.wait_ge(ACTC.h, ACTC.n)
        ten.wait_ge(VEC.h, VEC.n)
        for f in range(F):
            ten.matmul(psA[:, :C], oneb[:], xb[:, f, :], start=(f == 0), stop=(f == F - 1))
        i = None
        for f in range(F):
            i = ten.matmul(psB[:, :C], oneb[:], xq[:, f, :], start=(f == 0), stop=(f == F - 1))
        TEN.inc(i)

    def ln_stats_fin(C, psA, psB, psR, psM):
        mu = rows_mu[:, 0:C]
        rs = rows_rs[:, 0:C]
        tp = rows_tp[:, 0:C]
        vec.wait_ge(TEN.h, TEN.n)
        vec.tensor_scalar(mu, psA[:, :C], 1.0 / E, None, ALU.mult)
        vec.tensor_scalar(tp, psB[:, :C], 1.0 / E, None, ALU.mult)
        vec.tensor_tensor(rs, mu, mu, ALU.mult)
        vec.tensor_tensor(tp, tp, rs, ALU.subtract)
        act.wait_ge(VEC.h, VEC.n)
        act.activation(tp, tp, AF.Ln, bias=epsv[0:1, 0:1])
        act.activation(rs, tp, AF.Exp, scale=-0.5)
        vec.wait_ge(ACTC.h, ACTC.n)
        vec.tensor_tensor(mu, mu, rs, ALU.mult)
        ten.wait_ge(VEC.h, VEC.n)
        ten.matmul(psR[:, :C], oner[:], rs, start=True, stop=True)
        TEN.inc(ten.matmul(psM[:, :C], oner[:], mu, start=True, stop=True))
        vec.wait_ge(TEN.h, TEN.n)
        psRb = psR[:, :C].unsqueeze(1).broadcast_to([128, F, C])
        psMb = psM[:, :C].unsqueeze(1).broadcast_to([128, F, C])
        return psRb, psMb

    def ln_stats(src3, C, xb, xq, psA, psB, psR, psM):
        ln_stats_top(src3, C, xb, xq, psA, psB)
        return ln_stats_fin(C, psA, psB, psR, psM)

    with contextlib.ExitStack() as stg, scope("OLN"):
        olx = stg.enter_context(nc.sbuf_tensor(uname("olx"), [128, F, CB], F32))
        olo = stg.enter_context(nc.sbuf_tensor(uname("olo"), [128, F, CB], F32))
        olob = stg.enter_context(nc.sbuf_tensor(uname("olob"), [128, F, CB], BF16))
        olxb = stg.enter_context(nc.sbuf_tensor(uname("olxb"), [128, F, CB], BF16))
        olxq = stg.enter_context(nc.sbuf_tensor(uname("olxq"), [128, F, CB], BF16))
        gksb = stg.enter_context(nc.sbuf_tensor(uname("gksb"), [128, F, NE], F32))
        ol_psA = stg.enter_context(nc.psum_tensor(uname("ol_psA"), [1, CB], F32))
        ol_psB = stg.enter_context(nc.psum_tensor(uname("ol_psB"), [1, CB], F32))
        ol_psR = stg.enter_context(nc.psum_tensor(uname("ol_psR"), [128, CB], F32))
        ol_psM = stg.enter_context(nc.psum_tensor(uname("ol_psM"), [128, CB], F32))
        ol_psG = stg.enter_context(nc.psum_tensor(uname("ol_psG"), [NE, CB], F32))
        stage_start()
        ld(gksb[:], gk_h.rearrange("(k p) m -> p k m", p=128))
        for ci in range(S // CB):
            c0 = ci * CB
            sync.wait_ge(VEC.h, VEC.n)
            sync.wait_ge(ACTC.h, ACTC.n)
            ld(olx[:], xT_s[:, :, c0:c0 + CB].transpose([1, 0, 2]))
            for e in (act, ten, vec):
                e.wait_ge(LD.h, LD.n)
            psRb, psMb = ln_stats(olx[:], CB, olxb[:], olxq[:],
                                  ol_psA, ol_psB, ol_psR, ol_psM)
            vec.tensor_tensor(olo[:], olx[:], psRb, ALU.mult)
            VEC.inc(vec.tensor_tensor(olo[:], olo[:], psMb, ALU.subtract))
            act.wait_ge(VEC.h, VEC.n)
            ACTC.inc(act.activation(olob[:], olo[:], AF.Copy))
            ten.wait_ge(VEC.h, VEC.n)
            i = None
            for f in range(F):
                i = ten.matmul(ol_psG[:, :CB], gksb[:, f, :], olo[:, f, :],
                               start=(f == 0), stop=(f == F - 1))
            TEN.inc(i)
            vec.wait_ge(TEN.h, TEN.n)
            VEC.inc(vec.tensor_copy(logits[:, c0:c0 + CB], ol_psG[:, :CB]))
            sync.wait_ge(VEC.h, VEC.n)
            sync.wait_ge(ACTC.h, ACTC.n)
            ld(xa_s[:, :, c0:c0 + CB].transpose([1, 0, 2]), olo[:])
            ld(xnb_s[:, :, c0:c0 + CB].transpose([1, 0, 2]), olob[:])
    bar()
    tap3("xn", xa_s)

    for l in range(NL):
        CAs = [128] * 16
        C0s = [sum(CAs[:j]) for j in range(len(CAs))]
        NCH = len(CAs)
        WMAX = 129
        with contextlib.ExitStack() as stg, scope(f"A{l}"):
            aw_sb = stg.enter_context(nc.sbuf_tensor(uname("aw_sb"), [128, F, 3 * E], BF16))
            abv = stg.enter_context(nc.sbuf_tensor(uname("abv"), [128, 24], F32))
            l1wv = stg.enter_context(nc.sbuf_tensor(uname("l1wv"), [128, 8], F32))
            l1bv = stg.enter_context(nc.sbuf_tensor(uname("l1bv"), [128, 8], F32))
            ax = stg.enter_context(nc.sbuf_tensor(uname("ax"), [128, F, WMAX], F32))
            axb = stg.enter_context(nc.sbuf_tensor(uname("axb"), [128, F, WMAX], BF16))
            axq = stg.enter_context(nc.sbuf_tensor(uname("axq"), [128, F, WMAX], BF16))
            lnb = stg.enter_context(nc.sbuf_tensor(uname("lnb"), [128, F, WMAX], BF16))
            qkv2 = [stg.enter_context(nc.sbuf_tensor(uname("qkvA"), [128, 24, WMAX], F32)),
                    stg.enter_context(nc.sbuf_tensor(uname("qkvB"), [128, 24, WMAX], F32))]
            p0b2 = [stg.enter_context(nc.sbuf_tensor(uname("p0bA"), [128, F, 128], BF16)),
                    stg.enter_context(nc.sbuf_tensor(uname("p0bB"), [128, F, 128], BF16))]
            def pair(nm):
                return [stg.enter_context(nc.sbuf_tensor(uname(nm), [128, F, 128], F32))
                        for _ in range(2)]
            tq3_2 = pair("tq3")
            tk3_2 = pair("tk3")
            tv3_2 = pair("tv3")
            trlq_2 = pair("trlq")
            trlk_2 = pair("trlk")
            trlv_2 = pair("trlv")
            tq1 = stg.enter_context(nc.sbuf_tensor(uname("tq1"), [128, F, 128], F32))
            tk1 = stg.enter_context(nc.sbuf_tensor(uname("tk1"), [128, F, 128], F32))
            tm = stg.enter_context(nc.sbuf_tensor(uname("tm"), [128, F, 128], F32))
            tt = stg.enter_context(nc.sbuf_tensor(uname("tt"), [128, F, 128], F32))
            att = stg.enter_context(nc.sbuf_tensor(uname("att"), [128, F, 128], F32))
            qk0 = stg.enter_context(nc.sbuf_tensor(uname("qk0"), [128, F, 128], F32))
            qk1 = stg.enter_context(nc.sbuf_tensor(uname("qk1"), [128, F, 128], F32))
            qk2 = stg.enter_context(nc.sbuf_tensor(uname("qk2"), [128, F, 128], F32))
            qk3 = stg.enter_context(nc.sbuf_tensor(uname("qk3"), [128, F, 128], F32))
            a_psA = stg.enter_context(nc.psum_tensor(uname("a_psA"), [1, WMAX], F32))
            a_psB = stg.enter_context(nc.psum_tensor(uname("a_psB"), [1, WMAX], F32))
            a_psR = stg.enter_context(nc.psum_tensor(uname("a_psR"), [128, WMAX], F32))
            a_psM = stg.enter_context(nc.psum_tensor(uname("a_psM"), [128, WMAX], F32))
            a_psQ = [stg.enter_context(nc.psum_tensor(uname("a_psQ"), [128, WMAX], F32))
                     for _ in range(4)]
            stage_start()
            wld(aw_sb[:], aw_h[l].rearrange("(k p) m -> p k m", p=128))
            ld(abv[:], ab_h[l][:])
            ld(l1wv[:], l1w_h[l][:])
            ld(l1bv[:], l1b_h[l][:])
            psQ = [Bank(p) for p in a_psQ]
            evac_marks = [0, 0]
            prep_marks = [0, 0]
            cos_marks = [0, 0]
            p0_mark = [0, 0]
            lnb_mark = [0]

            def front_top(j):
                Cj = CAs[j]
                c0 = C0s[j]
                hl = 1 if j > 0 else 0
                W = Cj + hl
                sync.wait_ge(VEC.h, lnb_mark[0])
                sync.wait_ge(ACTC.h, stats_copy_mark[0])
                ld(ax[:, :, :W], xa_s[:, :, c0 - hl:c0 + Cj].transpose([1, 0, 2]))
                for e in (act, ten):
                    e.wait_ge(LD.h, LD.n)
                    e.wait_ge(WL.h, WL.n)
                ln_stats_top(ax[:, :, :W], W, axb[:, :, :W], axq[:, :, :W],
                             a_psA, a_psB)

            def front_dve(j):
                Cj = CAs[j]
                hl = 1 if j > 0 else 0
                W = Cj + hl
                psRb, psMb = ln_stats_fin(W, a_psA, a_psB, a_psR, a_psM)
                gam = l1wv[:].unsqueeze(2).broadcast_to([128, F, W])
                bet = l1bv[:].unsqueeze(2).broadcast_to([128, F, W])
                vec.tensor_tensor(ax[:, :, :W], ax[:, :, :W], psRb, ALU.mult)
                vec.tensor_tensor(ax[:, :, :W], ax[:, :, :W], psMb, ALU.subtract)
                vec.tensor_tensor(ax[:, :, :W], ax[:, :, :W], gam, ALU.mult)
                vec.tensor_tensor(lnb[:, :, :W], ax[:, :, :W], bet, ALU.add)
                lnb_mark[0] = VEC.n

            def front_mm(j):
                Cj = CAs[j]
                c0 = C0s[j]
                hl = 1 if j > 0 else 0
                W = Cj + hl
                par = j % 2
                qkv = qkv2[par]
                ten.wait_ge(VEC.h, lnb_mark[0])
                for m in range(24):
                    bk = psQ[m % 4]
                    bk.wait()
                    i = None
                    for k in range(F):
                        i = ten.matmul(bk.ps[:, :W], aw_sb[:, k, m * 128:(m + 1) * 128],
                                       lnb[:, k, :W], start=(k == 0), stop=(k == F - 1))
                    TEN.inc(i)
                    act.wait_ge(TEN.h, TEN.n)
                    if m == 0 and p0_mark[par]:
                        act.wait_ge(VEC.h, p0_mark[par])
                    act.activation(qkv[:, m, :W], bk.ps[:, :W], AF.Identity,
                                   bias=abv[:, m:m + 1])
                    bk.a = ACTC.n
                evac_marks[par] = ACTC.n
                q0 = qkv[:, 0:8, hl:W]
                k0 = qkv[:, 8:16, hl:W]
                v0 = qkv[:, 16:24, hl:W]

                def act_cos(dst, src_):
                    act.activation(att[:, :, :Cj], src_, AF.Sin, scale=0.5)
                    act.activation(att[:, :, :Cj], att[:, :, :Cj], AF.Square)
                    act.activation(dst[:, :, :Cj], att[:, :, :Cj], AF.Identity,
                                   scale=-2.0, bias=1.0)

                def act_roll(dst, zb):
                    act.copy(dst[64:128, :, :Cj], zb[0:64, :, :])
                    act.copy(dst[0:64, 1:8, :Cj], zb[64:128, 0:7, :])
                    act.copy(dst[0:64, 0:1, :Cj], zb[64:128, 7:8, :])

                if p0_mark[par]:
                    act.wait_ge(VEC.h, p0_mark[par])
                act_cos(tq3_2[par], q0)
                act_cos(tk3_2[par], k0)
                cos_marks[par] = ACTC.n
                act_roll(trlq_2[par], q0)
                act_roll(trlk_2[par], k0)
                act_cos(tv3_2[par], v0)
                act_roll(trlv_2[par], v0)
                prep_marks[par] = ACTC.n

            def quat_dve(i, half):
                Ci = CAs[i]
                c0 = C0s[i]
                hl = 1 if i > 0 else 0
                W = Ci + hl
                par = i % 2
                qkv = qkv2[par]
                p0b = p0b2[par]
                q0 = qkv[:, 0:8, hl:W]
                k0 = qkv[:, 8:16, hl:W]
                v0 = qkv[:, 16:24, hl:W]
                tq3, tk3, tv3 = tq3_2[par], tk3_2[par], tv3_2[par]
                trlq, trlk, trlv = trlq_2[par], trlk_2[par], trlv_2[par]
                T = lambda t: t[:, :, :Ci]

                def diff(dst, base):
                    if hl:
                        vec.tensor_tensor(T(dst), qkv[:, base:base + 8, 1:W],
                                          qkv[:, base:base + 8, 0:W - 1], ALU.subtract)
                    else:
                        vec.tensor_tensor(dst[:, :, 1:Ci], qkv[:, base:base + 8, 1:Ci],
                                          qkv[:, base:base + 8, 0:Ci - 1], ALU.subtract)
                        vec.memset(dst[:, :, 0:1], 0.0)

                def roll_into(dst, zb):
                    vec.tensor_copy(dst[64:128, :, :Ci], zb[0:64, :, :])
                    vec.tensor_copy(dst[0:64, 1:8, :Ci], zb[64:128, 0:7, :])
                    vec.tensor_copy(dst[0:64, 0:1, :Ci], zb[64:128, 7:8, :])

                if half == 0:
                    vec.wait_ge(ACTC.h, evac_marks[par])
                    diff(tq1, 0)
                    diff(tk1, 8)
                    vec.tensor_tensor(T(tm), q0, k0, ALU.mult)
                    vec.tensor_tensor(T(tt), T(tq1), T(tk1), ALU.mult)
                    vec.tensor_tensor(T(qk0), T(tm), T(tt), ALU.subtract)
                    vec.wait_ge(ACTC.h, prep_marks[par])
                    vec.tensor_tensor(T(qk1), T(trlq), T(trlk), ALU.mult)
                    vec.tensor_tensor(T(qk0), T(qk0), T(qk1), ALU.subtract)
                    vec.tensor_tensor(T(tt), T(tq3), T(tk3), ALU.mult)
                    vec.tensor_tensor(T(qk0), T(qk0), T(tt), ALU.subtract)
                    vec.tensor_tensor(T(qk1), q0, T(tk1), ALU.mult)
                    vec.tensor_tensor(T(tt), T(tq1), k0, ALU.mult)
                    vec.tensor_tensor(T(qk1), T(qk1), T(tt), ALU.add)
                    vec.tensor_tensor(T(tt), T(trlq), T(tk3), ALU.mult)
                    vec.tensor_tensor(T(qk1), T(qk1), T(tt), ALU.add)
                    vec.tensor_tensor(T(tt), T(tq3), T(trlk), ALU.mult)
                    vec.tensor_tensor(T(qk1), T(qk1), T(tt), ALU.subtract)
                else:
                    vec.tensor_tensor(T(qk2), q0, T(trlk), ALU.mult)
                    vec.tensor_tensor(T(tt), T(tq1), T(tk3), ALU.mult)
                    vec.tensor_tensor(T(qk2), T(qk2), T(tt), ALU.subtract)
                    vec.tensor_tensor(T(tt), T(trlq), k0, ALU.mult)
                    vec.tensor_tensor(T(qk2), T(qk2), T(tt), ALU.add)
                    vec.tensor_tensor(T(tt), T(tq3), T(tk1), ALU.mult)
                    vec.tensor_tensor(T(qk2), T(qk2), T(tt), ALU.add)
                    vec.tensor_tensor(T(qk3), q0, T(tk3), ALU.mult)
                    vec.tensor_tensor(T(tt), T(tq1), T(trlk), ALU.mult)
                    vec.tensor_tensor(T(qk3), T(qk3), T(tt), ALU.add)
                    vec.tensor_tensor(T(tt), T(trlq), T(tk1), ALU.mult)
                    vec.tensor_tensor(T(qk3), T(qk3), T(tt), ALU.subtract)
                    vec.tensor_tensor(T(tt), T(tq3), k0, ALU.mult)
                    vec.tensor_tensor(T(qk3), T(qk3), T(tt), ALU.add)
                    diff(tq1, 16)
                    vec.tensor_tensor(T(tm), T(qk0), v0, ALU.mult)
                    vec.tensor_tensor(T(tt), T(qk1), T(tq1), ALU.mult)
                    vec.tensor_tensor(T(tm), T(tm), T(tt), ALU.subtract)
                    vec.tensor_tensor(T(tt), T(qk2), T(trlv), ALU.mult)
                    vec.tensor_tensor(T(tm), T(tm), T(tt), ALU.subtract)
                    vec.tensor_tensor(T(tt), T(qk3), T(tv3), ALU.mult)
                    vec.wait_ge(LD.h, LD.n)
                    vec.tensor_tensor(p0b[:, :, :Ci], T(tm), T(tt), ALU.subtract)
                    p0_mark[par] = VEC.n
                    sync.wait_ge(VEC.h, VEC.n)
                    ld(p0_s[:, :, c0:c0 + Ci].transpose([1, 0, 2]), p0b[:, :, :Ci])

            front_top(0)
            front_dve(0)
            front_mm(0)
            for i in range(NCH):
                if i + 1 < NCH:
                    front_top(i + 1)
                quat_dve(i, 0)
                if i + 1 < NCH:
                    front_dve(i + 1)
                quat_dve(i, 1)
                if i + 1 < NCH:
                    front_mm(i + 1)
        bar()
        if l == 0:
            tap3("p0_0", p0_s)
            if ln1_s is not None:
                tap3("ln1_0", ln1_s)
            if qkv_s is not None:
                tap3("qkv_0", qkv_s)

        with contextlib.ExitStack() as stg, scope(f"B{l}"):
            Qsb = stg.enter_context(nc.sbuf_tensor(uname("Qsb"), [128, F, E], BF16))
            pw_sb = stg.enter_context(nc.sbuf_tensor(uname("pw_sb"), [128, F, E], BF16))
            wp_sb = stg.enter_context(nc.sbuf_tensor(uname("wp_sb"), [128, F, E], BF16))
            pbv = stg.enter_context(nc.sbuf_tensor(uname("pbv"), [128, 8], F32))
            modb = stg.enter_context(nc.sbuf_tensor(uname("modb"), [128, S], F32))
            bp0 = stg.enter_context(nc.sbuf_tensor(uname("bp0"), [128, F, CB], BF16))
            bct = stg.enter_context(nc.sbuf_tensor(uname("bct"), [128, F, CB], F32))
            bxa = stg.enter_context(nc.sbuf_tensor(uname("bxa"), [128, F, CB], F32))
            btp = stg.enter_context(nc.sbuf_tensor(uname("btp"), [128, F, CB], BF16))
            b_ps0 = stg.enter_context(nc.psum_tensor(uname("b_ps0"), [128, CB], F32))
            b_ps1 = stg.enter_context(nc.psum_tensor(uname("b_ps1"), [128, CB], F32))
            stage_start()
            ld(Qsb[:], cQ_h.rearrange("(k p) m -> p k m", p=128))
            ld(modb[:], cMOD_h[:])
            wld(pw_sb[:], pw_h[l].rearrange("(k p) m -> p k m", p=128))
            ld(pbv[:], pb_h[l][:])
            psB2 = [Bank(b_ps0), Bank(b_ps1)]
            ten.wait_ge(WL.h, WL.n)
            ten.wait_ge(LD.h, LD.n)
            for m in range(F):
                for half in range(2):
                    bk = psB2[half]
                    bk.wait()
                    n0 = half * 512
                    i = None
                    for k in range(F):
                        i = ten.matmul(bk.ps[:, :512], Qsb[:, k, m * 128:(m + 1) * 128],
                                       pw_sb[:, k, n0:n0 + 512],
                                       start=(k == 0), stop=(k == F - 1))
                    TEN.inc(i)
                    act.wait_ge(TEN.h, TEN.n)
                    bk.a = ACTC.inc(act.activation(
                        wp_sb[:, m, n0:n0 + 512], bk.ps[:, :512], AF.Copy))
            for ci in range(S // CB):
                c0 = ci * CB
                sync.wait_ge(VEC.h, VEC.n)
                sync.wait_ge(LD.h, LD.n)
                ld(bp0[:], p0_s[:, :, c0:c0 + CB].transpose([1, 0, 2]))
                ld(bct[:], cCT_h.rearrange("(f p) s -> p f s", p=128)[:, :, c0:c0 + CB])
                ld(bxa[:], xa_s[:, :, c0:c0 + CB].transpose([1, 0, 2]))
                for e in (ten, vec):
                    e.wait_ge(LD.h, LD.n)
                for m in range(F):
                    bk = psB2[m % 2]
                    bk.wait()
                    i = None
                    for k in range(F):
                        i = ten.matmul(bk.ps[:, :CB], Qsb[:, k, m * 128:(m + 1) * 128],
                                       bp0[:, k, :], start=(k == 0), stop=(k == F - 1))
                    TEN.inc(i)
                    vec.wait_ge(TEN.h, TEN.n)
                    bk.v = VEC.inc(vec.tensor_tensor(
                        btp[:, m, :], bk.ps[:, :CB], bct[:, m, :], ALU.mult))
                ten.wait_ge(VEC.h, VEC.n)
                for m in range(F):
                    bk = psB2[m % 2]
                    bk.wait()
                    i = None
                    for k in range(F):
                        i = ten.matmul(bk.ps[:, :CB], wp_sb[:, k, m * 128:(m + 1) * 128],
                                       btp[:, k, :], start=(k == 0), stop=(k == F - 1))
                    TEN.inc(i)
                    vec.wait_ge(TEN.h, TEN.n)
                    vec.tensor_tensor(bct[:, m, :], bk.ps[:, :CB],
                                      modb[:, c0:c0 + CB], ALU.mult)
                    bk.v = VEC.inc(vec.scalar_tensor_tensor(
                        bxa[:, m, :], bct[:, m, :], pbv[:, m:m + 1], bxa[:, m, :],
                        ALU.add, ALU.add))
                sync.wait_ge(VEC.h, VEC.n)
                ld(xa_s[:, :, c0:c0 + CB].transpose([1, 0, 2]), bxa[:])
        bar()
        if l == 0:
            tap3("xa_a0", xa_s)

        with contextlib.ExitStack() as stg, scope(f"D{l}"):
            fw_sb = stg.enter_context(nc.sbuf_tensor(uname("fw_sb"), [128, F, 4 * E], BF16))
            mwb = [stg.enter_context(nc.sbuf_tensor(uname("mwb"), [128, 32, 128], BF16))
                   for _ in range(3)]
            fbv = stg.enter_context(nc.sbuf_tensor(uname("fbv"), [128, 32], F32))
            mbv = stg.enter_context(nc.sbuf_tensor(uname("mbv"), [128, 8], F32))
            l2wv = stg.enter_context(nc.sbuf_tensor(uname("l2wv"), [128, 8], F32))
            l2bv = stg.enter_context(nc.sbuf_tensor(uname("l2bv"), [128, 8], F32))
            dx2 = [stg.enter_context(nc.sbuf_tensor(uname("dxA"), [128, F, CB], F32)),
                   stg.enter_context(nc.sbuf_tensor(uname("dxB"), [128, F, CB], F32))]
            dln2 = [stg.enter_context(nc.sbuf_tensor(uname("dlnA"), [128, F, CB], BF16)),
                    stg.enter_context(nc.sbuf_tensor(uname("dlnB"), [128, F, CB], BF16))]
            dxb = stg.enter_context(nc.sbuf_tensor(uname("dxb"), [128, F, CB], BF16))
            dgs = stg.enter_context(nc.sbuf_tensor(uname("dgs"), [128, F, CB], F32))
            dg = stg.enter_context(nc.sbuf_tensor(uname("dg"), [128, 32, CB], BF16))
            d_psA = stg.enter_context(nc.psum_tensor(uname("d_psA"), [1, CB], F32))
            d_psB = stg.enter_context(nc.psum_tensor(uname("d_psB"), [1, CB], F32))
            d_psR = stg.enter_context(nc.psum_tensor(uname("d_psR"), [128, CB], F32))
            d_psM = stg.enter_context(nc.psum_tensor(uname("d_psM"), [128, CB], F32))
            d_psF = [stg.enter_context(nc.psum_tensor(uname("d_psF"), [128, CB], F32))
                     for _ in range(4)]
            stage_start()
            wld(fw_sb[:], fw_h[l].rearrange("(k p) m -> p k m", p=128))
            mwT = mw_h[l].rearrange("(k p) m -> p k m", p=128)
            mw_cons = [0, 0, 0]
            mw_tok = [0, 0, 0]
            mw_next = [0]

            def mw_prefetch(upto):
                total = NCH_D * F if True else 0
                while mw_next[0] < min(upto, S // CB * F):
                    g = mw_next[0]
                    slot = g % 3
                    pool.wait_ge(TEN.h, mw_cons[slot])
                    wld(mwb[slot][:], mwT[:, :, (g % F) * 128:(g % F) * 128 + 128])
                    mw_tok[slot] = WL.n
                    mw_next[0] = g + 1
            ld(fbv[:], fb_h[l][:])
            ld(mbv[:], mb_h[l][:])
            ld(l2wv[:], l2w_h[l][:])
            ld(l2bv[:], l2b_h[l][:])
            psD = [Bank(p) for p in d_psF]
            NCH_D = S // CB
            dln_mark = [0]
            mpe_mark = [0]

            def d_top(j):
                par = j % 2
                c0 = j * CB
                sync.wait_ge(VEC.h, VEC.n)
                sync.wait_ge(ACTC.h, ACTC.n)
                sync.wait_ge(LD.h, LD.n)
                ld(dx2[par][:], xa_s[:, :, c0:c0 + CB].transpose([1, 0, 2]))
                for e in (act, ten):
                    e.wait_ge(LD.h, LD.n)
                    e.wait_ge(WL.h, WL.n)
                ln_stats_top(dx2[par][:], CB, dxb[:], dln2[par][:], d_psA, d_psB)

            def d_ln(j):
                par = j % 2
                psRb, psMb = ln_stats_fin(CB, d_psA, d_psB, d_psR, d_psM)
                gam = l2wv[:].unsqueeze(2).broadcast_to([128, F, CB])
                bet = l2bv[:].unsqueeze(2).broadcast_to([128, F, CB])
                vec.tensor_tensor(dgs[:], dx2[par][:], psRb, ALU.mult)
                vec.tensor_tensor(dgs[:], dgs[:], psMb, ALU.subtract)
                vec.tensor_tensor(dgs[:], dgs[:], gam, ALU.mult)
                vec.tensor_tensor(dln2[par][:], dgs[:], bet, ALU.add)
                dln_mark[0] = VEC.n

            def d_fc(i):
                par = i % 2
                mw_prefetch(i * F + 2)
                ten.wait_ge(VEC.h, dln_mark[0])
                for mo in range(32):
                    bk = psD[mo % 4]
                    bk.wait()
                    ii = None
                    for k in range(F):
                        ii = ten.matmul(bk.ps[:, :CB], fw_sb[:, k, mo * 128:(mo + 1) * 128],
                                        dln2[par][:, k, :], start=(k == 0), stop=(k == F - 1))
                    TEN.inc(ii)
                    act.wait_ge(TEN.h, TEN.n)
                    if mo == 0 and mpe_mark[0]:
                        act.wait_ge(TEN.h, mpe_mark[0])
                    act.activation(dg[:, mo, :], bk.ps[:, :CB], AF.Gelu,
                                   bias=fbv[:, mo:mo + 1])
                    bk.a = ACTC.n

            def d_mproj(i):
                par = i % 2
                c0 = i * CB
                ten.wait_ge(ACTC.h, ACTC.n)
                for m2 in range(F):
                    g = i * F + m2
                    mw_prefetch(g + 3)
                    slot = g % 3
                    bk = psD[m2 % 4]
                    bk.wait()
                    ten.wait_ge(WL.h, mw_tok[slot])
                    ii = None
                    for mo in range(32):
                        ii = ten.matmul(bk.ps[:, :CB], mwb[slot][:, mo, 0:128],
                                        dg[:, mo, :], start=(mo == 0), stop=(mo == 31))
                    mw_cons[slot] = TEN.inc(ii)
                    vec.wait_ge(TEN.h, TEN.n)
                    vec.scalar_tensor_tensor(
                        dx2[par][:, m2, :], bk.ps[:, :CB], mbv[:, m2:m2 + 1],
                        dx2[par][:, m2, :], ALU.add, ALU.add)
                    bk.v = VEC.n
                mpe_mark[0] = mw_cons[(i * F + F - 1) % 3]
                sync.wait_ge(VEC.h, VEC.n)
                ld(xa_s[:, :, c0:c0 + CB].transpose([1, 0, 2]), dx2[par][:])

            d_top(0)
            d_ln(0)
            for i in range(NCH_D):
                if i + 1 < NCH_D:
                    d_top(i + 1)
                d_fc(i)
                if i + 1 < NCH_D:
                    d_ln(i + 1)
                d_mproj(i)
        bar()
        if l == 0:
            tap3("xa_0", xa_s)
        if l == NL - 1:
            tap3("xa_3", xa_s)

    with contextlib.ExitStack() as stg, scope("R"):
        gbv = stg.enter_context(nc.sbuf_tensor(uname("gbv"), [NE, 1], F32))
        sfc = stg.enter_context(nc.sbuf_tensor(uname("sfc"), [NE, S], F32))
        sfcT = stg.enter_context(nc.sbuf_tensor(uname("sfcT"), [128, 16, NE], F32))
        r_gs = stg.enter_context(nc.sbuf_tensor(uname("r_gs"), [128, 16, 4], F32))
        r_t1 = stg.enter_context(nc.sbuf_tensor(uname("r_t1"), [128, 16, 4], F32))
        r_t2 = stg.enter_context(nc.sbuf_tensor(uname("r_t2"), [128, 16, 4], F32))
        r_r1 = stg.enter_context(nc.sbuf_tensor(uname("r_r1"), [128, 16], F32))
        r_r2 = stg.enter_context(nc.sbuf_tensor(uname("r_r2"), [128, 16], F32))
        r_ms = stg.enter_context(nc.sbuf_tensor(uname("r_ms"), [128, 16, NE], F32))
        r_tc = stg.enter_context(nc.sbuf_tensor(uname("r_tc"), [128, 16, NE], F32))
        r_cm = stg.enter_context(nc.sbuf_tensor(uname("r_cm"), [128, 16, NE], F32))
        r_ps = stg.enter_context(nc.psum_tensor(uname("r_ps"), [128, 16], F32))
        r_psT = stg.enter_context(nc.psum_tensor(uname("r_psT"), [16, 128], F32))
        stage_start()
        ld(gbv[:], gb_h[:])
        act.wait_ge(LD.h, LD.n)
        ACTC.inc(act.activation(sfc[:], logits[:], AF.Sigmoid))
        vec.wait_ge(ACTC.h, ACTC.n)
        VEC.inc(vec.tensor_scalar(sfc[:], sfc[:], gbv[:], None, ALU.add))
        ten.wait_ge(VEC.h, VEC.n)
        for t in range(16):
            if t > 0:
                ten.wait_ge(VEC.h, VEC.n)
            TEN.inc(ten.transpose(r_ps[:], sfc[:, t * 128:(t + 1) * 128], ident[0:16, 0:16]))
            vec.wait_ge(TEN.h, TEN.n)
            VEC.inc(vec.tensor_copy(sfcT[:, t, :], r_ps[:]))
        if "sfcT" in taps:
            sync.wait_ge(VEC.h, VEC.n)
            ld(taps["sfcT"][:], sfcT[:].rearrange("p t e -> p (t e)"))
        s4 = sfcT[:].rearrange("p t (g j) -> p t g j", g=4)
        vec.reduce_max(r_gs[:], s4, axis=AX.X)
        m1b = r_gs[:].unsqueeze(3).broadcast_to([128, 16, 4, 4])
        t4 = r_ms[:].rearrange("p t (g j) -> p t g j", g=4)
        vec.tensor_tensor(t4, s4, m1b, ALU.is_ge)
        vec.scalar_tensor_tensor(t4, t4, NEG, s4, ALU.mult, ALU.add)
        vec.reduce_max(r_t1[:], t4, axis=AX.X)
        vec.tensor_tensor(r_gs[:], r_gs[:], r_t1[:], ALU.add)
        vec.reduce_max(r_r1[:], r_gs[:], axis=AX.X)
        g1b = r_r1[:].unsqueeze(2).broadcast_to([128, 16, 4])
        vec.tensor_tensor(r_t1[:], r_gs[:], g1b, ALU.is_ge)
        vec.scalar_tensor_tensor(r_t1[:], r_t1[:], NEG, r_gs[:], ALU.mult, ALU.add)
        vec.reduce_max(r_r2[:], r_t1[:], axis=AX.X)
        r2b = r_r2[:].unsqueeze(2).broadcast_to([128, 16, 4])
        vec.tensor_tensor(r_t1[:], r_gs[:], r2b, ALU.is_ge)
        mb4 = r_t1[:].unsqueeze(3).broadcast_to([128, 16, 4, 4])
        vec.tensor_tensor(r_ms[:].rearrange("p t (g j) -> p t g j", g=4),
                          s4, mb4, ALU.mult)
        if "ms" in taps:
            sync.wait_ge(VEC.h, VEC.n)
            ld(taps["ms"][:], r_ms[:].rearrange("p t e -> p (t e)"))
        vec.tensor_copy(r_tc[:], r_ms[:])
        for it in range(4):
            vec.reduce_max(r_r1[:], r_tc[:], axis=AX.X)
            if it < 3:
                r1b = r_r1[:].unsqueeze(2).broadcast_to([128, 16, NE])
                vec.tensor_tensor(r_cm[:], r_tc[:], r1b, ALU.is_ge)
                vec.scalar_tensor_tensor(r_tc[:], r_cm[:], NEG, r_tc[:],
                                         ALU.mult, ALU.add)
        m4b = r_r1[:].unsqueeze(2).broadcast_to([128, 16, NE])
        vec.tensor_tensor(r_cm[:], r_ms[:], m4b, ALU.is_ge)
        vec.tensor_tensor(r_cm[:], r_cm[:], r_ms[:], ALU.mult)
        vec.reduce_sum(r_r1[:], r_cm[:], axis=AX.X)
        vec.tensor_scalar(r_r1[:], r_r1[:], 1e-20, None, ALU.add)
        vec.reciprocal(r_r2[:], r_r1[:])
        vec.tensor_scalar(r_r2[:], r_r2[:], 2.5, None, ALU.mult)
        wb = r_r2[:].unsqueeze(2).broadcast_to([128, 16, NE])
        vec.tensor_tensor(r_cm[:], r_cm[:], wb, ALU.mult)
        if "rcm" in taps:
            sync.wait_ge(VEC.h, VEC.n)
            ld(taps["rcm"][:], r_cm[:].rearrange("p t e -> p (t e)"))
            sync.wait_ge(LD.h, LD.n)
        ten.wait_ge(VEC.h, VEC.n)
        for t in range(16):
            if t > 0:
                ten.wait_ge(VEC.h, VEC.n)
            TEN.inc(ten.transpose(r_psT[:], r_cm[:, t, :], ident[:]))
            vec.wait_ge(TEN.h, TEN.n)
            VEC.inc(vec.tensor_copy(sfc[:, t * 128:(t + 1) * 128], r_psT[:]))
        sync.wait_ge(VEC.h, VEC.n)
        ld(comb_s[:], sfc[:, :])
    bar()
    if "comb" in taps:
        pool.wait_ge(LD.h, LD.n)
        wld(taps["comb"][:], comb_s[:])
        pool.wait_ge(WL.h, WL.n)
        bar()

    with contextlib.ExitStack() as stg, scope("SH"):
        swg_sb = stg.enter_context(nc.sbuf_tensor(uname("swg_sb"), [128, F, E], BF16))
        swu_sb = stg.enter_context(nc.sbuf_tensor(uname("swu_sb"), [128, F, E], BF16))
        swd_sb = stg.enter_context(nc.sbuf_tensor(uname("swd_sb"), [128, F, E], BF16))
        shx = stg.enter_context(nc.sbuf_tensor(uname("shx"), [128, F, CB], BF16))
        sha = stg.enter_context(nc.sbuf_tensor(uname("sha"), [128, F, CB], F32))
        shg = stg.enter_context(nc.sbuf_tensor(uname("shg"), [128, F, CB], BF16))
        shi = stg.enter_context(nc.sbuf_tensor(uname("shi"), [128, F, CB], BF16))
        sh_ps0 = stg.enter_context(nc.psum_tensor(uname("sh_ps0"), [128, CB], F32))
        sh_ps1 = stg.enter_context(nc.psum_tensor(uname("sh_ps1"), [128, CB], F32))
        stage_start()
        wld(swg_sb[:], swg_h.rearrange("(k p) m -> p k m", p=128))
        wld(swu_sb[:], swu_h.rearrange("(k p) m -> p k m", p=128))
        wld(swd_sb[:], swd_h.rearrange("(k p) m -> p k m", p=128))
        psS = [Bank(sh_ps0), Bank(sh_ps1)]
        for ci in range(S // CB):
            c0 = ci * CB
            sync.wait_ge(VEC.h, VEC.n)
            sync.wait_ge(ACTC.h, ACTC.n)
            sync.wait_ge(LD.h, LD.n)
            ld(shx[:], xnb_s[:, :, c0:c0 + CB].transpose([1, 0, 2]))
            ld(sha[:], xa_s[:, :, c0:c0 + CB].transpose([1, 0, 2]))
            for e in (act, ten, vec):
                e.wait_ge(LD.h, LD.n)
                e.wait_ge(WL.h, WL.n)
            for m in range(F):
                bk = psS[m % 2]
                bk.wait()
                i = None
                for k in range(F):
                    i = ten.matmul(bk.ps[:, :CB], swg_sb[:, k, m * 128:(m + 1) * 128],
                                   shx[:, k, :], start=(k == 0), stop=(k == F - 1))
                TEN.inc(i)
                act.wait_ge(TEN.h, TEN.n)
                bk.a = ACTC.inc(act.activation(shg[:, m, :], bk.ps[:, :CB], AF.Silu))
            ten.wait_ge(ACTC.h, ACTC.n)
            for m in range(F):
                bk = psS[m % 2]
                bk.wait()
                i = None
                for k in range(F):
                    i = ten.matmul(bk.ps[:, :CB], swu_sb[:, k, m * 128:(m + 1) * 128],
                                   shx[:, k, :], start=(k == 0), stop=(k == F - 1))
                TEN.inc(i)
                vec.wait_ge(TEN.h, TEN.n)
                bk.v = VEC.inc(vec.tensor_tensor(
                    shi[:, m, :], bk.ps[:, :CB], shg[:, m, :], ALU.mult))
            ten.wait_ge(VEC.h, VEC.n)
            for m in range(F):
                bk = psS[m % 2]
                bk.wait()
                i = None
                for k in range(F):
                    i = ten.matmul(bk.ps[:, :CB], swd_sb[:, k, m * 128:(m + 1) * 128],
                                   shi[:, k, :], start=(k == 0), stop=(k == F - 1))
                TEN.inc(i)
                vec.wait_ge(TEN.h, TEN.n)
                bk.v = VEC.inc(vec.tensor_tensor(
                    sha[:, m, :], bk.ps[:, :CB], sha[:, m, :], ALU.add))
            sync.wait_ge(VEC.h, VEC.n)
            ld(xa_s[:, :, c0:c0 + CB].transpose([1, 0, 2]), sha[:])
    bar()

    with contextlib.ExitStack() as stg, scope("EX"):
        ewr = [stg.enter_context(nc.sbuf_tensor(uname("ewr"), [128, 2, F, 512], BF16))
               for _ in range(2)]
        eur = [stg.enter_context(nc.sbuf_tensor(uname("eur"), [128, 2, F, 512], BF16))
               for _ in range(2)]
        edr = [stg.enter_context(nc.sbuf_tensor(uname("edr"), [128, 2, 4, E], BF16))
               for _ in range(2)]
        ece = stg.enter_context(nc.sbuf_tensor(uname("ece"), [128, 2, CB], F32))
        exn = stg.enter_context(nc.sbuf_tensor(uname("exn"), [128, F, CB], BF16))
        exa = stg.enter_context(nc.sbuf_tensor(uname("exa"), [128, F, CB], F32))
        esg = stg.enter_context(nc.sbuf_tensor(uname("esg"), [128, 4, CB], BF16))
        eic = stg.enter_context(nc.sbuf_tensor(uname("eic"), [128, 2, 4, CB], BF16))
        e_ps = [stg.enter_context(nc.psum_tensor(uname("e_ps"), [128, CB], F32))
                for _ in range(4)]
        stage_start()
        psE = [Bank(p) for p in e_ps]
        ring_cons = [0, 0]
        ring_tok = [0, 0]

        def ex_load(g):
            slot = g % 2
            s = g % 8
            pool.wait_ge(TEN.h, ring_cons[slot])
            for j in range(2):
                e_id = s * 2 + j
                wld(ewr[slot][:, j, :, :], wg_h[e_id].rearrange("(k p) m -> p k m", p=128))
                wld(eur[slot][:, j, :, :], wu_h[e_id].rearrange("(k p) m -> p k m", p=128))
                wld(edr[slot][:, j, :, :], wd_h[e_id].rearrange("(k p) m -> p k m", p=128))
            ring_tok[slot] = WL.n

        ex_load(0)
        NCHX = S // CB
        for ci in range(NCHX):
            c0 = ci * CB
            sync.wait_ge(VEC.h, VEC.n)
            sync.wait_ge(ACTC.h, ACTC.n)
            sync.wait_ge(LD.h, LD.n)
            ld(exn[:], xnb_s[:, :, c0:c0 + CB].transpose([1, 0, 2]))
            ld(exa[:], xa_s[:, :, c0:c0 + CB].transpose([1, 0, 2]))
            xtok = LD.n
            for e in (act, ten, vec):
                e.wait_ge(LD.h, xtok)
            for s in range(8):
                g = ci * 8 + s
                slot = g % 2
                if g + 1 < NCHX * 8:
                    ex_load(g + 1)
                for j in range(2):
                    e_id = s * 2 + j
                    sync.wait_ge(VEC.h, VEC.n)
                    ld(ece[:, j, :], comb_s[e_id:e_id + 1, c0:c0 + CB]
                       .broadcast_to([128, CB]))
                etok = LD.n
                vec.wait_ge(LD.h, etok)
                ten.wait_ge(WL.h, ring_tok[slot])
                last_mm = 0
                for j in range(2):
                    for m in range(4):
                        bk = psE[(j * 4 + m) % 4]
                        bk.wait()
                        i = None
                        for k in range(F):
                            i = ten.matmul(bk.ps[:, :CB],
                                           ewr[slot][:, j, k, m * 128:(m + 1) * 128],
                                           exn[:, k, :], start=(k == 0),
                                           stop=(k == F - 1))
                        TEN.inc(i)
                        act.wait_ge(TEN.h, TEN.n)
                        act.activation(esg[:, m, :], bk.ps[:, :CB], AF.Silu)
                        bk.a = ACTC.n
                    for m in range(4):
                        bk = psE[(j * 4 + m) % 4]
                        bk.wait()
                        i = None
                        for k in range(F):
                            i = ten.matmul(bk.ps[:, :CB],
                                           eur[slot][:, j, k, m * 128:(m + 1) * 128],
                                           exn[:, k, :], start=(k == 0),
                                           stop=(k == F - 1))
                        TEN.inc(i)
                        vec.wait_ge(TEN.h, TEN.n)
                        vec.tensor_tensor(eic[:, j, m, :], bk.ps[:, :CB],
                                          esg[:, m, :], ALU.mult)
                        vec.tensor_tensor(eic[:, j, m, :], eic[:, j, m, :],
                                          ece[:, j, :], ALU.mult)
                        bk.v = VEC.n
                    act.wait_ge(VEC.h, VEC.n)
                ten.wait_ge(VEC.h, VEC.n)
                for m2 in range(F):
                    bk = psE[m2 % 4]
                    bk.wait()
                    i = None
                    for j in range(2):
                        for k in range(4):
                            i = ten.matmul(bk.ps[:, :CB],
                                           edr[slot][:, j, k, m2 * 128:(m2 + 1) * 128],
                                           eic[:, j, k, :],
                                           start=(j == 0 and k == 0),
                                           stop=(j == 1 and k == 3))
                    last_mm = TEN.inc(i)
                    vec.wait_ge(TEN.h, TEN.n)
                    vec.tensor_tensor(exa[:, m2, :], bk.ps[:, :CB],
                                      exa[:, m2, :], ALU.add)
                    bk.v = VEC.n
                ring_cons[slot] = last_mm
            sync.wait_ge(VEC.h, VEC.n)
            ld(xa_s[:, :, c0:c0 + CB].transpose([1, 0, 2]), exa[:])
    bar()
    tap3("moe", xa_s)

    with contextlib.ExitStack() as stg, scope("FIN"):
        fxa = stg.enter_context(nc.sbuf_tensor(uname("fxa"), [128, F, 128], F32))
        fx = stg.enter_context(nc.sbuf_tensor(uname("fx"), [128, F, 128], F32))
        frow = stg.enter_context(nc.sbuf_tensor(uname("frow"), [128, E], F32))
        f_ps = stg.enter_context(nc.psum_tensor(uname("f_ps"), [128, 128], F32))
        stage_start()
        for t in range(S // 128):
            sync.wait_ge(VEC.h, VEC.n)
            ld(fxa[:], xa_s[:, :, t * 128:(t + 1) * 128].transpose([1, 0, 2]))
            ld(fx[:], xT_s[:, :, t * 128:(t + 1) * 128].transpose([1, 0, 2]))
            vec.wait_ge(LD.h, LD.n)
            VEC.inc(vec.tensor_tensor(fxa[:], fxa[:], fx[:], ALU.add))
            ten.wait_ge(VEC.h, VEC.n)
            for f in range(F):
                if f > 0:
                    ten.wait_ge(VEC.h, VEC.n)
                TEN.inc(ten.transpose(f_ps[:], fxa[:, f, :], ident[:]))
                vec.wait_ge(TEN.h, TEN.n)
                VEC.inc(vec.tensor_copy(frow[:, f * 128:(f + 1) * 128], f_ps[:]))
            sync.wait_ge(VEC.h, VEC.n)
            ld(out_h[t * 128:(t + 1) * 128, :], frow[:])
        sync.wait_ge(LD.h, LD.n)
    bar()

    stack.close()
    return nc


_PROGRAM_CACHE = {}


def kernel(**inputs):
    consts = _build_constants()
    perm = _perm_qkv()

    x = np.ascontiguousarray(np.asarray(inputs["L0_attn_x"], np.float32))
    aw = np.asarray(inputs["L0_attn_attn_w"], np.float32)
    ab = np.asarray(inputs["L0_attn_attn_b"], np.float32)
    base = {}
    for l in range(NL):
        base[f"aw{l}"] = np.ascontiguousarray(aw[l][:, perm])
        base[f"ab{l}"] = _cols(ab[l][perm], 24)
        base[f"pw{l}"] = np.ascontiguousarray(np.asarray(inputs["L0_attn_aproj_w"][l], np.float32))
        base[f"pb{l}"] = _cols(inputs["L0_attn_aproj_b"][l], 8)
        base[f"fw{l}"] = np.ascontiguousarray(np.asarray(inputs["L0_attn_fc_w"][l], np.float32))
        base[f"fb{l}"] = _cols(inputs["L0_attn_fc_b"][l], 32)
        base[f"mw{l}"] = np.ascontiguousarray(np.asarray(inputs["L0_attn_mproj_w"][l], np.float32))
        base[f"mb{l}"] = _cols(inputs["L0_attn_mproj_b"][l], 8)
        base[f"l1w{l}"] = _cols(inputs["L0_attn_ln1_w"][l], 8)
        base[f"l1b{l}"] = _cols(inputs["L0_attn_ln1_b"][l], 8)
        base[f"l2w{l}"] = _cols(inputs["L0_attn_ln2_w"][l], 8)
        base[f"l2b{l}"] = _cols(inputs["L0_attn_ln2_b"][l], 8)
    base["gk"] = np.ascontiguousarray(np.asarray(inputs["L0_ffn_gate_kernel"], np.float32))
    base["gb"] = np.ascontiguousarray(np.asarray(inputs["L0_ffn_gate_bias"], np.float32).reshape(NE, 1))
    base["wg"] = np.ascontiguousarray(np.asarray(inputs["L0_ffn_w_gate"], np.float32))
    base["wu"] = np.ascontiguousarray(np.asarray(inputs["L0_ffn_w_up"], np.float32))
    base["wd"] = np.ascontiguousarray(np.asarray(inputs["L0_ffn_w_down"], np.float32))
    base["swg"] = np.ascontiguousarray(np.asarray(inputs["L0_ffn_sw_gate"], np.float32))
    base["swu"] = np.ascontiguousarray(np.asarray(inputs["L0_ffn_sw_up"], np.float32))
    base["swd"] = np.ascontiguousarray(np.asarray(inputs["L0_ffn_sw_down"], np.float32))
    base.update(consts)

    nc = build_program()
    in_maps = []
    for c in range(8):
        m = dict(base)
        m["x"] = np.ascontiguousarray(x[c])
        in_maps.append(m)
    res = run_bass_kernel_spmd(nc, in_maps, core_ids=list(range(8)))
    out = np.stack([np.asarray(r["out"], np.float32) for r in res.results], axis=0)
    if DEBUG_TAPS:
        kernel._taps = [{k: np.asarray(v) for k, v in r.items() if k.startswith("tap_")}
                        for r in res.results]
    return out
